# version 25
# baseline (speedup 1.0000x reference)
import numpy as np
import ml_dtypes
from contextlib import ExitStack

import concourse.bass as bass
import concourse.mybir as mybir
from concourse.bass_utils import run_bass_kernel_spmd

F32 = mybir.dt.float32
BF16 = mybir.dt.bfloat16
AF = mybir.ActivationFunctionType
ALU = mybir.AluOpType
BF = ml_dtypes.bfloat16

P = 128
TC = 512
L = 4096
NCH = L // TC
NG = 4
SC = TC + 1
EPS = 1e-5
ENGS = ("q", "t", "v", "s", "g")
LN_ENG = "v"
INPLACE_SP = True


class PB:

    def __init__(self):
        self.ops = []
        self.wmap = {}
        self.rmap = {}
        self.last_eng = {}
        self.last_chan = {}
        self.pending = {}
        self.chans = set()

    def op(self, eng, fn, rd=(), wr=(), deps=(), force=(), chan=None):
        rec = {"eng": eng, "fn": fn, "chan": chan or eng,
               "dma": chan is not None, "idx": len(self.ops),
               "deps": [x for x in deps if x is not None],
               "force": [x for x in force if x is not None],
               "need_inc": False, "semval": None}
        if chan is not None:
            self.chans.add(chan)
        for k in rd:
            w = self.wmap.get(k)
            if w is not None:
                rec["deps"].append(w)
            self.rmap.setdefault(k, []).append(rec)
        for k in wr:
            w = self.wmap.get(k)
            if w is not None:
                rec["deps"].append(w)
            rec["deps"].extend(self.rmap.get(k, ()))
            self.wmap[k] = rec
            self.rmap[k] = []
        bd = self.pending.pop(eng, None)
        if bd:
            rec["deps"].extend(bd)
        rec["deps"] = [x for x in rec["deps"] if x is not rec]
        self.ops.append(rec)
        self.last_eng[eng] = rec
        if chan is not None:
            self.last_chan[chan] = rec
        return rec

    def dma(self, fn, rd=(), wr=(), deps=(), chan="dm", eng="q"):
        return self.op(eng, fn, rd, wr, deps, chan=chan)

    def barrier(self):
        snap = list(self.last_eng.values()) + list(self.last_chan.values())
        for e in ENGS:
            self.pending[e] = list(snap)

    def finalize(self):
        for op in self.ops:
            for x in op["deps"]:
                if x["dma"] or x["eng"] != op["eng"]:
                    x["need_inc"] = True
            for x in op["force"]:
                x["need_inc"] = True
            if op["dma"]:
                op["need_inc"] = True
        cnt = {}
        for op in self.ops:
            if op["need_inc"]:
                cnt[op["chan"]] = cnt.get(op["chan"], 0) + (16 if op["dma"] else 1)
                op["semval"] = cnt[op["chan"]]
        self.totals = cnt

    def replay(self, eng, e, sems):
        waited = {}
        for op in self.ops:
            if op["eng"] != eng:
                continue
            forced = set(id(x) for x in op["force"])
            for x in op["deps"] + op["force"]:
                if (x["eng"] == op["eng"] and not x["dma"]
                        and id(x) not in forced):
                    continue
                val = x["semval"]
                if val is None:
                    continue
                if waited.get(x["chan"], 0) < val:
                    e.wait_ge(sems[x["chan"]], val)
                    waited[x["chan"]] = val
            inst = op["fn"](e)
            if op["need_inc"]:
                inst.then_inc(sems[op["chan"]], 16 if op["dma"] else 1)


def build_program(phases=(1, 2, 3, 4), dbg_streams=False):
    nc = bass.Bass("TRN2", target_bir_lowering=False)
    d = {}

    def din(name, shape, dt=F32):
        d[name] = nc.dram_tensor(name, list(shape), dt, kind="ExternalInput").ap()
        return d[name]

    for nm, shape, dt in [
            ("xT", (1024, L), F32), ("wioT", (1024, 4096), F32),
            ("wxT", (128, 528), F32), ("cw0", (128, 64), F32),
            ("cb0", (128, 16), F32), ("wdt", (128, 16), F32),
            ("bdt", (128, 16), F32), ("alog", (128, 256), F32),
            ("dpar", (128, 16), F32), ("woT", (2048, 1024), F32),
            ("w1T", (1024, 4096), F32), ("b1", (128, 32), F32),
            ("w2T", (4096, 1024), F32), ("b2", (128, 8), F32),
            ("scl0", (1, 1), F32), ("nw1", (128, 8), F32),
            ("winL1", (1024, 2048), F32), ("wresL1", (1024, 2048), F32),
            ("cw1", (128, 64), F32), ("cb1", (128, 16), F32),
            ("woutL1", (2048, 1024), F32), ("wfcT", (1024, 4096), F32),
            ("wprT", (4096, 1024), F32), ("sel33", (33, 33 * 128), BF16),
            ("ones1", (1, 128), BF16), ("onesc", (128, 1), BF16),
            ("iden", (128, 128), BF16)]:
        din(nm, shape, dt)

    out_ap = nc.dram_tensor("out", [1024, L], F32, kind="ExternalOutput").ap()
    skind = "ExternalOutput" if dbg_streams else "Internal"
    X = {1: nc.dram_tensor("X1", [1024, L], F32, kind=skind).ap(),
         2: nc.dram_tensor("X2", [1024, L], F32, kind=skind).ap(),
         3: nc.dram_tensor("X3", [1024, L], F32, kind=skind).ap()}
    WS = {}
    WGEO = {"wioT": (512, 8), "woT": (128, 16), "w1T": (512, 8),
            "w2T": (128, 32), "winL1": (512, 8), "wresL1": (512, 8),
            "woutL1": (128, 16), "wfcT": (512, 8), "wprT": (128, 32)}
    for nm, shape in [("wioT", (1024, 4096)), ("woT", (2048, 1024)),
                      ("w1T", (1024, 4096)), ("w2T", (4096, 1024)),
                      ("winL1", (1024, 2048)), ("wresL1", (1024, 2048)),
                      ("woutL1", (2048, 1024)), ("wfcT", (1024, 4096)),
                      ("wprT", (4096, 1024))]:
        WS[nm] = nc.dram_tensor("ws_" + nm, [P, shape[0] * shape[1] // P],
                                BF16, kind="Internal").ap()
    lscr = nc.dram_tensor("lscr", [128, 128], BF16, kind="Internal").ap()

    pb = PB()
    ctx = ExitStack()

    def mk(c):
        def sbf(name, shape, dt=F32):
            return c.enter_context(nc.sbuf_tensor(name, list(shape), dt))

        def psf(name, shape, dt=F32):
            return c.enter_context(nc.psum_tensor(name, list(shape), dt))
        return sbf, psf

    sb, ps_ = mk(ctx)

    cw0_sb = sb("cw0_sb", (P, 64))
    cb0_sb = sb("cb0_sb", (P, 16))
    wdt_sb = sb("wdt_sb", (P, 16))
    bdt_sb = sb("bdt_sb", (P, 16))
    dpar_sb = sb("dpar_sb", (P, 16))
    a_sb = sb("a_sb", (P, 256))
    cw1_sb = sb("cw1_sb", (P, 64))
    cb1_sb = sb("cb1_sb", (P, 16))
    nw_sb = sb("nw_sb", (P, 8))
    b1_sb = sb("b1_sb", (P, 32))
    b2_sb = sb("b2_sb", (P, 8))
    sel_sb = sb("sel_sb", (33, 33 * 128), BF16)
    ones_sb = sb("ones_sb", (1, 128), BF16)
    onesc_sb = sb("onesc_sb", (P, 1), BF16)
    onesc32 = sb("onesc32", (P, 1))
    iden_sb = sb("iden_sb", (P, 128), BF16)
    sclrow = sb("sclrow", (1, 1))
    scl_sb = sb("scl_sb", (P, 1))
    hstate = sb("hstate", (P, 256))
    halo0 = sb("halo0", (P, 48), BF16)
    halo1 = sb("halo1", (P, 48), BF16)
    cnst = sb("cnst", (P, 3))
    wxT_sb = sb("wxT_sb", (P, 16 * 33), BF16)

    xin = sb("xin", (P, 8 * TC))
    x_bf = sb("x_bf", (P, 8 * TC), BF16)
    xn_bf = sb("xnsq", (P, 8 * TC), BF16)
    xsq_bf = xn_bf
    oa = sb("oa", (P, 8 * TC))
    wsl = sb("wsl", (P, 4096), BF16)
    wslB = sb("wslB", (P, 4096), BF16)
    tmpf = sb("tmpf", (P, TC))
    mrow = sb("mrow", (1, TC))
    vrow = sb("vrow", (1, TC))
    rrow = sb("rrow", (1, TC))
    m_bfr = sb("m_bfr", (1, TC), BF16)
    r_bfr = sb("r_bfr", (1, TC), BF16)

    s_ps = ps_("s_ps", (1, TC))
    sq_ps = ps_("sq_ps", (1, TC))
    rep_ps = [ps_("rep_ps0", (P, TC)), ps_("rep_ps1", (P, TC))]
    mm_ps = [ps_("mm_ps0", (P, TC)), ps_("mm_ps1", (P, TC))]

    KP = "params"

    with ExitStack() as p0ctx:
        sb0, _ = mk(p0ctx)
        pb.dma(lambda e: e.dma_start(out=cw0_sb[:], in_=d["cw0"]),
               wr=[KP, "cw0_sb"], chan="p0")
        pb.dma(lambda e: e.dma_start(out=cw1_sb[:], in_=d["cw1"]),
               wr=[KP, "cw1_sb"], chan="p0")
        pb.dma(lambda e: e.dma_start(out=a_sb[:], in_=d["alog"]),
               wr=[KP, "a_sb"], chan="p0")
        for dst, nm, key in [(cb0_sb, "cb0", "cb0_sb"), (wdt_sb, "wdt", "wdt_sb"),
                             (bdt_sb, "bdt", "bdt_sb"), (dpar_sb, "dpar", "dpar_sb"),
                             (cb1_sb, "cb1", "cb1_sb"), (nw_sb, "nw1", "nw_sb"),
                             (b1_sb, "b1", "b1_sb"), (b2_sb, "b2", "b2_sb")]:
            pb.dma(lambda e, d0=dst, s0=d[nm]: e.dma_start(
                out=d0[:], in_=s0), wr=[KP, key], chan="p0")
        pb.dma(lambda e: e.dma_start(out=sel_sb[:], in_=d["sel33"]),
               wr=[KP, "sel_sb"], chan="p0")
        pb.dma(lambda e: e.dma_start(out=ones_sb[:], in_=d["ones1"]),
               wr=[KP, "ones_sb"], chan="p0")
        pb.dma(lambda e: e.dma_start(out=onesc_sb[:], in_=d["onesc"]),
               wr=[KP, "onesc_sb"], chan="p0")
        pb.op("v", lambda e: e.memset(onesc32[:], 1.0), wr=["onesc32"])
        pb.dma(lambda e: e.dma_start(out=iden_sb[:], in_=d["iden"]),
               wr=[KP, "iden_sb"], chan="p0")
        pb.dma(lambda e: e.dma_start(out=sclrow[:], in_=d["scl0"]),
               wr=[KP, "sclrow"], chan="p0")
        pb.op("s", lambda e: e.activation(a_sb[:], a_sb[:], AF.Exp),
              rd=["a_sb"], wr=["a_sb"])
        pb.op("v", lambda e: e.tensor_scalar_mul(a_sb[:], a_sb[:], -1.0),
              rd=["a_sb"], wr=["a_sb"])
        pb.dma(lambda e: e.dma_start(out=scl_sb[0:1, :], in_=sclrow[:]),
               rd=["sclrow"], wr=["scl_sb"], chan="p0")
        width = 1
        while width < P:
            w2 = min(width, P - width)
            pb.dma(lambda e, a=width, b=w2: e.dma_start(
                out=scl_sb[a:a + b, :], in_=scl_sb[0:b, :]),
                rd=["scl_sb"], wr=["scl_sb"], chan="p0")
            width += w2
        pb.op("v", lambda e: e.memset(cnst[:, 0:1], EPS), wr=["cnst"])
        pb.op("v", lambda e: e.memset(cnst[:, 1:2], -1.0), wr=["cnst"])
        pb.op("v", lambda e: e.memset(cnst[:, 2:3], -0.5), wr=["cnst"])
        pb.op("v", lambda e: e.memset(hstate[:], 0.0), wr=["hstate"])
        pb.op("v", lambda e: e.memset(halo0[:], 0.0), wr=["halo0"])
        pb.op("v", lambda e: e.memset(halo1[:], 0.0), wr=["halo1"])

        wxstage = sb0("wxstage", (P, 16 * 33))
        pb.dma(lambda e: e.dma_start(out=wxstage[:], in_=d["wxT"]),
            wr=["wxstage"], chan="p0")
        pb.op("v", lambda e: e.tensor_copy(wxT_sb[:], wxstage[:]),
              rd=["wxstage"], wr=["wxT_sb"])

        need = {1: ["wioT", "woT"], 2: ["w1T", "w2T"],
                3: ["winL1", "wresL1", "woutL1"], 4: ["wfcT", "wprT"]}
        cast_list = []
        for ph in phases:
            cast_list.extend(need[ph])
        jobs = []
        for nm in cast_list:
            srcT, dst = d[nm], WS[nm]
            rows, cols = srcT.tensor.shape
            for r0 in range(0, rows, P):
                for c0 in range(0, cols, 2048):
                    cw = min(2048, cols - c0)
                    jobs.append((srcT, dst, nm, r0, c0, cw))
        for ji, (srcT, dst, nm, r0, c0, cw) in enumerate(jobs):
            pr = ji % 2
            ob = pr * 2048
            gc, nk = WGEO[nm]
            kk = r0 // P
            ng_ = cw // gc
            base = (c0 // gc) * (nk * gc) + kk * gc
            pb.dma(lambda e, s=srcT, r=r0, c0=c0, cw=cw, ob=ob: e.dma_start(
                out=oa[:, ob:ob + cw], in_=s[r:r + P, c0:c0 + cw]),
                wr=[("oa", "c", pr)], chan="wca" if pr == 0 else "wcb",
                eng="s" if pr == 0 else "g")
            pb.op("v", lambda e, cw=cw, ob=ob: e.tensor_copy(
                x_bf[:, ob:ob + cw], oa[:, ob:ob + cw]),
                rd=[("oa", "c", pr)], wr=[("xbf", "c", pr)])
            pb.dma(lambda e, ds=dst, g0=c0 // gc, kk=kk, ng_=ng_, gc=gc,
                   nk=nk, cw=cw, ob=ob: e.dma_start(
                out=ds.rearrange("p (g k c) -> p g k c", k=nk, c=gc)
                [:, g0:g0 + ng_, kk:kk + 1, :],
                in_=x_bf[:, ob:ob + cw]
                .rearrange("p (g x c) -> p g x c", x=1, c=gc)),
                rd=[("xbf", "c", pr)], wr=[("ws", nm)],
                chan="wcc" if pr == 0 else "wcd")
        pb.op("v", lambda e: e.memset(oa[:, 0:1], 0.0),
              rd=[("oa", "c", 0), ("oa", "c", 1),
                  ("xbf", "c", 0), ("xbf", "c", 1)],
              wr=["oa"])

    def ln_stats(src, src_key, ntiles, rms_only=False, sqb=None, sqk=None):
        nt = ntiles * TC
        if sqb is None:
            sqb, sqk = xsq_bf, "xnsq"
        pb.op("s", lambda e, sqb=sqb: e.activation(
            sqb[:, 0:nt], src[:, 0:nt], AF.Square),
              rd=[src_key], wr=[sqk])
        for k in range(ntiles):
            if not rms_only:
                pb.op("t", lambda e, k=k: e.matmul(
                    s_ps[:], onesc32[:], src[:, k * TC:(k + 1) * TC],
                    start=(k == 0), stop=(k == ntiles - 1)),
                    rd=[src_key, "onesc32"], wr=["s_ps"])
            pb.op("t", lambda e, k=k, sqb=sqb: e.matmul(
                sq_ps[:], onesc_sb[:], sqb[:, k * TC:(k + 1) * TC],
                start=(k == 0), stop=(k == ntiles - 1)),
                rd=[sqk, "onesc_sb"], wr=["sq_ps"])
        dd = float(ntiles * P)
        if rms_only:
            pb.op("v", lambda e: e.tensor_single_scalar(
                vrow[:], sq_ps[:], 1.0 / dd, ALU.mult),
                rd=["sq_ps"], wr=["vrow"])
        else:
            pb.op("v", lambda e: e.tensor_single_scalar(
                mrow[:], s_ps[:], 1.0 / dd, ALU.mult),
                rd=["s_ps"], wr=["mrow"])
            pb.op("v", lambda e: e.tensor_single_scalar(
                vrow[:], sq_ps[:], 1.0 / dd, ALU.mult),
                rd=["sq_ps"], wr=["vrow"])
            pb.op("v", lambda e: e.tensor_tensor(
                rrow[:], mrow[:], mrow[:], ALU.mult),
                rd=["mrow"], wr=["rrow"])
            pb.op("v", lambda e: e.tensor_tensor(
                vrow[:], vrow[:], rrow[:], ALU.subtract),
                rd=["vrow", "rrow"], wr=["vrow"])
            pb.op("v", lambda e: e.tensor_copy(m_bfr[:], mrow[:]),
                  rd=["mrow"], wr=["m_bfr"])
        pb.op("s", lambda e: e.activation(vrow[:], vrow[:], AF.Ln,
                                          bias=cnst[0:1, 0:1]),
              rd=["vrow", "cnst"], wr=["vrow"])
        pb.op("s", lambda e: e.activation(rrow[:], vrow[:], AF.Exp,
                                          scale=cnst[0:1, 2:3]),
              rd=["vrow", "cnst"], wr=["rrow"])
        pb.op("v", lambda e: e.tensor_copy(r_bfr[:], rrow[:]),
              rd=["rrow"], wr=["r_bfr"])
        if not rms_only:
            pb.op("t", lambda e: e.matmul(rep_ps[0][:], ones_sb[:], m_bfr[:],
                                          start=True, stop=True),
                  rd=["m_bfr", "ones_sb"], wr=["rep0"])
        pb.op("t", lambda e: e.matmul(rep_ps[1][:], ones_sb[:], r_bfr[:],
                                      start=True, stop=True),
              rd=["r_bfr", "ones_sb"], wr=["rep1"])

    def ln_apply(src, src_key, dst, dst_key, ntiles, res=None, res_key=None):
        for k in range(ntiles):
            pb.op("v", lambda e, k=k: e.tensor_tensor(
                tmpf[:], src[:, k * TC:(k + 1) * TC], rep_ps[0][:],
                ALU.subtract), rd=[src_key, "rep0"], wr=["tmpf"])
            if res is None:
                pb.op("v", lambda e, k=k: e.tensor_tensor(
                    dst[:, k * TC:(k + 1) * TC], tmpf[:], rep_ps[1][:],
                    ALU.mult), rd=["tmpf", "rep1"], wr=[dst_key])
            else:
                pb.op("v", lambda e, k=k: e.tensor_tensor(
                    tmpf[:], tmpf[:], rep_ps[1][:], ALU.mult),
                    rd=["tmpf", "rep1"], wr=["tmpf"])
                pb.op("v", lambda e, k=k: e.tensor_tensor(
                    dst[:, k * TC:(k + 1) * TC], tmpf[:],
                    res[:, k * TC:(k + 1) * TC], ALU.add),
                    rd=["tmpf", res_key], wr=[dst_key])

    def load_chunk(src_stream, skey, c, buf=None, bkey="xin", chan="xia"):
        if buf is None:
            buf = xin
        pb.dma(lambda e, t0=c * TC, b=buf: e.dma_start(
            out=b[:].rearrange("p (k t) -> p k t", k=8),
            in_=src_stream[:, t0:t0 + TC].rearrange("(k p) t -> p k t", p=P)),
            rd=[skey], wr=[bkey], chan=chan)

    def store_chunk(dst_stream, dkey, c, src, src_key):
        pb.dma(lambda e, t0=c * TC: e.dma_start(
            out=dst_stream[:, t0:t0 + TC].rearrange("(k p) t -> p k t", p=P),
            in_=src[:].rearrange("p (k t) -> p k t", k=8)),
            rd=[src_key], wr=[dkey], chan="st")

    WB = [(None, "wsla", "wa"), (None, "wslb", "wb")]

    def wslices(ws_name, col0, ncols, nk, par):
        buf, bkey, chan = WB[par]
        g = col0 // ncols
        pb.dma(lambda e, b=buf, g=g, w=nk * ncols: e.dma_start(
            out=b[:, 0:w], in_=WS[ws_name][:, g * w:(g + 1) * w]),
            rd=[("ws", ws_name)], wr=[bkey], chan=chan)
        return buf, bkey

    WB[0] = (wsl, "wsla", "wa")
    WB[1] = (wslB, "wslb", "wb")

    active = [ph for ph in (1, 2, 3, 4) if ph in phases]

    def stream_of(ph):
        i = active.index(ph)
        src = (d["xT"], "xT") if i == 0 else (X[active[i - 1]], ("X", active[i - 1]))
        dst = (out_ap, "OUT") if i == len(active) - 1 else (X[ph], ("X", ph))
        return src, dst


    if 1 in phases:
        p1ctx = ExitStack()
        sb1, ps1 = mk(p1ctx)
        (srcS, srcK), (dstS, dstK) = stream_of(1)
        xc_bf = sb1("p1_xc", (P, 16 * TC), BF16)
        sz_bf = sb1("p1_sz", (P, 16 * TC), BF16)
        pad0 = sb1("p1_pad0", (P, TC + 3), BF16)
        pad = [pad0, pad0]
        accv = [tmpf, tmpf]
        xp_bf = sb1("p1_xp", (33, TC), BF16)
        brep = sb1("p1_brep", (P, 16 * TC), BF16)
        crep = sb1("p1_crep", (P, 16 * TC), BF16)
        xinB_1 = sb1("p1_xinB", (P, 8 * TC))
        dap0 = sb1("p1_dap", (P, NG * SC))
        dx0 = sb1("p1_dx", (P, NG * SC))
        dapA = dapB = dap0
        dxA = dxB = dx0
        spx = tmpf
        spt = sb1("p1_spt", (P, TC))
        delta = spt
        uu = sb1("p1_u", (P, TC))
        y_acc = sb1("p1_yacc", (P, TC))
        y_g = sb1("p1_yg", (P, 16 * TC), BF16)
        xp_ps = ps1("p1_xp_ps", (33, TC))
        dtx_ps = ps1("p1_dtx_ps", (P, TC))

        pb.op("v", lambda e: e.memset(dapA[:], 0.0), wr=[("dap", 0)])
        pb.op("v", lambda e: e.memset(dapB[:], 0.0), wr=[("dap", 1)])
        pb.op("v", lambda e: e.memset(dxA[:], 0.0), wr=[("dx", 0)])
        pb.op("v", lambda e: e.memset(dxB[:], 0.0), wr=[("dx", 1)])

        def emit_trees(job):
            s, ysl, ng = job
            for nn in (2, 1, 3):
                pb.op("g", lambda e, ysl=ysl, nn=nn: e.tensor_tensor(
                    ysl[:, 0:TC], ysl[:, 0:TC],
                    ysl[:, nn * TC:(nn + 1) * TC], ALU.add),
                    rd=[("oay", s)], wr=[("oay", s)])
            if ng == 0:
                pb.op("g", lambda e, ysl=ysl: e.tensor_copy(
                    y_acc[:], ysl[:, 0:TC]),
                    rd=[("oay", s)], wr=["y_acc"])
            else:
                pb.op("g", lambda e, ysl=ysl: e.tensor_tensor(
                    y_acc[:], y_acc[:], ysl[:, 0:TC], ALU.add),
                    rd=[("oay", s), "y_acc"], wr=["y_acc"])

        def p1_xbuf(c):
            return ((xin, "xin", "xia") if c % 2 == 0
                    else (xinB_1, "xinB", "xib"))

        def p1_front(c):
            xb, xk, xch = p1_xbuf(c)
            load_chunk(srcS, srcK, c, xb, xk, xch)
            ln_stats(xb, xk, 8)
            ln_apply(xb, xk, xn_bf, "xnsq", 8)

            for g in range(8):
                wb_, wk_ = wslices("wioT", g * 512, 512, 8, g % 2)
                for fi in range(4):
                    f = g * 4 + fi
                    for k in range(8):
                        pb.op("t", lambda e, fi=fi, k=k, f=f, wb_=wb_: e.matmul(
                            mm_ps[f % 2][:],
                            wb_[:, k * 512 + fi * 128: k * 512 + fi * 128 + 128],
                            xn_bf[:, k * TC:(k + 1) * TC],
                            start=(k == 0), stop=(k == 7)),
                            rd=[wk_, "xnsq"], wr=[("mm", f % 2)])
                    if f < 16:
                        pb.op("s", lambda e, f=f: e.activation(
                            pad[f % 2][:, 3:3 + TC], mm_ps[f % 2][:], AF.Copy),
                            rd=[("mm", f % 2)], wr=[("pad", 0)])
                        pb.op("v", lambda e, f=f: e.tensor_copy(
                            pad[f % 2][:, 0:3], halo0[:, f * 3:f * 3 + 3]),
                            rd=["halo0"], wr=[("pad", 0)])
                        pb.op("v", lambda e, f=f: e.tensor_scalar(
                            accv[f % 2][:], pad[f % 2][:, 0:TC],
                            cw0_sb[:, 4 * f:4 * f + 1], None, ALU.mult),
                            rd=[("pad", 0), "cw0_sb"], wr=["tmpf"])
                        for kk in range(1, 4):
                            pb.op("v", lambda e, f=f, kk=kk:
                                  e.scalar_tensor_tensor(
                                      accv[f % 2][:], pad[f % 2][:, kk:kk + TC],
                                      cw0_sb[:, 4 * f + kk:4 * f + kk + 1],
                                      accv[f % 2][:], ALU.mult, ALU.add),
                                  rd=[("pad", 0), "tmpf", "cw0_sb"],
                                  wr=["tmpf"])
                        pb.op("v", lambda e, f=f: e.tensor_copy(
                            halo0[:, f * 3:f * 3 + 3], pad[f % 2][:, TC:TC + 3]),
                            rd=[("pad", 0)], wr=["halo0"])
                        pb.op("s", lambda e, f=f: e.activation(
                            xc_bf[:, f * TC:(f + 1) * TC], accv[f % 2][:],
                            AF.Silu, bias=cb0_sb[:, f:f + 1]),
                            rd=["tmpf", "cb0_sb"], wr=[("xc", f)])
                    else:
                        pb.op("s", lambda e, f=f: e.activation(
                            sz_bf[:, (f - 16) * TC:(f - 15) * TC],
                            mm_ps[f % 2][:], AF.Silu),
                            rd=[("mm", f % 2)], wr=["sz_bf"])

            for k in range(16):
                pb.op("t", lambda e, k=k: e.matmul(
                    xp_ps[:], wxT_sb[:, k * 33:(k + 1) * 33],
                    xc_bf[:, k * TC:(k + 1) * TC],
                    start=(k == 0), stop=(k == 15)),
                    rd=[("xc", k), "wxT_sb"], wr=["xp_ps"])
            pb.op("v", lambda e: e.tensor_copy(xp_bf[:], xp_ps[:]),
                  rd=["xp_ps"], wr=["xp_bf"])
            for n in range(16):
                pb.op("t", lambda e, n=n: e.matmul(
                    rep_ps[0][:], sel_sb[:, n * P:(n + 1) * P], xp_bf[:],
                    start=True, stop=True),
                    rd=["xp_bf", "sel_sb"], wr=["rep0"])
                pb.op("v", lambda e, n=n: e.tensor_copy(
                    brep[:, n * TC:(n + 1) * TC], rep_ps[0][:]),
                    rd=["rep0"], wr=["brep"])
                pb.op("t", lambda e, n=n: e.matmul(
                    rep_ps[1][:], sel_sb[:, (16 + n) * P:(17 + n) * P], xp_bf[:],
                    start=True, stop=True),
                    rd=["xp_bf", "sel_sb"], wr=["rep1"])
                pb.op("v", lambda e, n=n: e.tensor_copy(
                    crep[:, n * TC:(n + 1) * TC], rep_ps[1][:]),
                    rd=["rep1"], wr=["crep"])
            pb.op("t", lambda e: e.matmul(
                dtx_ps[:], sel_sb[:, 32 * P:33 * P], xp_bf[:],
                start=True, stop=True),
                rd=["xp_bf", "sel_sb"], wr=["dtx_ps"])

        def p1_scan(c):
            for dt in range(16):
                pb.op("s", lambda e, dt=dt: e.activation(
                    spx[:], dtx_ps[:], AF.Identity,
                    scale=wdt_sb[:, dt:dt + 1], bias=bdt_sb[:, dt:dt + 1]),
                    rd=["dtx_ps", "wdt_sb", "bdt_sb"], wr=["tmpf"])
                pb.op("s", lambda e: e.activation(spt[:], spx[:], AF.Abs),
                      rd=["tmpf"], wr=["spt"])
                pb.op("s", lambda e: e.activation(spt[:], spt[:], AF.Exp,
                                                  scale=cnst[:, 1:2]),
                      rd=["spt", "cnst"], wr=["spt"])
                pb.op("s", lambda e: e.activation(uu[:], spt[:], AF.Ln,
                                                  bias=1.0),
                      rd=["spt"], wr=["uu"])
                pb.op("v", lambda e: e.scalar_tensor_tensor(
                    delta[:], spx[:], 0.0, uu[:], ALU.max, ALU.add),
                    rd=["tmpf", "uu"], wr=["spt"])
                pb.op("v", lambda e, dt=dt: e.tensor_tensor(
                    uu[:], delta[:], xc_bf[:, dt * TC:(dt + 1) * TC], ALU.mult),
                    rd=["spt", ("xc", dt)], wr=["uu"])
                pending = []
                for ng in range(16 // NG):
                    nb = ng * NG
                    it = dt * (16 // NG) + ng
                    par = 0
                    dap = dapA
                    dbxp = dxA
                    acts = []
                    for nn in range(NG):
                        acts.append(pb.op("s", lambda e, dt=dt, nn=nn, nb=nb,
                                          dap=dap: e.activation(
                            dap[:, nn * SC + 1: nn * SC + 1 + TC], delta[:],
                            AF.Exp,
                            scale=a_sb[:, dt * 16 + nb + nn:
                                       dt * 16 + nb + nn + 1]),
                            rd=["spt", "a_sb"], wr=[("dap", par)]))
                    bx = pb.op("v", lambda e, nb=nb, dbxp=dbxp: e.tensor_tensor(
                        dbxp[:].rearrange("p (n t) -> p n t", n=NG)
                        [:, :, 1:1 + TC],
                        uu[:, :].unsqueeze(1).broadcast_to([P, NG, TC]),
                        brep[:, nb * TC:(nb + NG) * TC]
                        .rearrange("p (n t) -> p n t", n=NG), ALU.mult),
                        rd=["uu", "brep"], wr=[("dx", par)])
                    stc = pb.op("s", lambda e, dt=dt, nb=nb, dbxp=dbxp:
                                e.activation(
                        dbxp[:, 0:NG * SC:SC],
                        hstate[:, dt * 16 + nb:dt * 16 + nb + NG], AF.Copy),
                        rd=["hstate"], wr=[("dx", par)])
                    scn = pb.op("v", lambda e, dap=dap, dbxp=dbxp:
                                e.tensor_tensor_scan(
                        dbxp[:], dap[:], dbxp[:], 0.0, ALU.mult, ALU.add),
                        rd=[("dap", par), ("dx", par)], wr=[("dx", par)],
                        force=[bx, stc])
                    pb.op("v", lambda e, dt=dt, nb=nb, dbxp=dbxp: e.tensor_copy(
                        hstate[:, dt * 16 + nb:dt * 16 + nb + NG],
                        dbxp[:, TC:NG * SC:SC]),
                        rd=[("dx", par)], wr=["hstate"], force=[scn])
                    s = ng % 2
                    ysl = oa[:, s * NG * TC:(s + 1) * NG * TC]
                    pb.op("v", lambda e, nb=nb, dbxp=dbxp, ysl=ysl:
                          e.tensor_tensor(
                        ysl.rearrange("p (n t) -> p n t", n=NG),
                        dbxp[:].rearrange("p (n t) -> p n t", n=NG)
                        [:, :, 1:1 + TC],
                        crep[:, nb * TC:(nb + NG) * TC]
                        .rearrange("p (n t) -> p n t", n=NG), ALU.mult),
                        rd=[("dx", par), "crep"],
                        wr=(["oa", ("oay", s)] if it < 2
                            else [("oay", s)]))
                    emit_trees((s, ysl, ng))
                pb.op("v", lambda e, dt=dt: e.scalar_tensor_tensor(
                    y_acc[:], xc_bf[:, dt * TC:(dt + 1) * TC],
                    dpar_sb[:, dt:dt + 1], y_acc[:], ALU.mult, ALU.add),
                    rd=[("xc", dt), "dpar_sb", "y_acc"], wr=["y_acc"])
                pb.op("v", lambda e, dt=dt: e.tensor_tensor(
                    y_g[:, dt * TC:(dt + 1) * TC], y_acc[:],
                    sz_bf[:, dt * TC:(dt + 1) * TC], ALU.mult),
                    rd=["y_acc", "sz_bf"], wr=["y_g"])

        def p1_tail(c):
            for o in range(8):
                wb_, wk_ = wslices("woT", o * 128, 128, 16, o % 2)
                for k in range(16):
                    pb.op("t", lambda e, o=o, k=k, wb_=wb_: e.matmul(
                        mm_ps[o % 2][:], wb_[:, k * 128:(k + 1) * 128],
                        y_g[:, k * TC:(k + 1) * TC],
                        start=(k == 0), stop=(k == 15)),
                        rd=[wk_, "y_g"], wr=[("mm", o % 2)])
                pb.op("s", lambda e, o=o: e.activation(
                    oa[:, o * TC:(o + 1) * TC], mm_ps[o % 2][:], AF.Copy),
                    rd=[("mm", o % 2)], wr=["oa"])
            xb, xk, xch = p1_xbuf(c)
            ln_stats(oa, "oa", 8)
            ln_apply(oa, "oa", oa, "oa", 8, res=xb, res_key=xk)
            store_chunk(dstS, dstK, c, oa, "oa")
        p1_front(0)
        for c in range(NCH):
            p1_scan(c)
            if c + 1 < NCH:
                p1_front(c + 1)
            p1_tail(c)
        pb.barrier()
        p1ctx.close()

    if 2 in phases:
        p2ctx = ExitStack()
        sb2, ps2 = mk(p2ctx)
        (srcS, srcK), (dstS, dstK) = stream_of(2)
        xinB_2 = sb2("p2_xinB", (P, 8 * TC))
        xn2 = sb2("p2_xn2", (P, 8 * TC), BF16)
        h_bf = sb2("p2_h", (P, 32 * TC), BF16)
        lstage = sb2("p2_lstage", (P, 128), BF16)
        ltr = sb2("p2_ltr", (P, 128), BF16)
        odd1_bf = sb2("p2_odd1", (P, 32), BF16)
        odd2_bf = sb2("p2_odd2", (P, 32), BF16)
        evn = [sb2(f"p2_evn{i}", (1, 1024), BF16) for i in range(8)]
        u1rows = sb2("p2_u1rows", (4, TC), BF16)
        u1r = sb2("p2_u1r", (1, 4 * TC), BF16)
        v_sb = sb2("p2_vsb", (4, 4 * TC), BF16)
        vr = sb2("p2_vr", (1, 16 * TC), BF16)
        u1_ps = ps2("p2_u1ps", (4, TC))
        v_ps = ps2("p2_vps", (4, TC))

        def p2_xbuf(c):
            return ((xin, "xin", "xia") if c % 2 == 0
                    else (xinB_2, "xinB", "xib"))

        def p2_nbuf(c):
            return (xn_bf, "xnsq") if c % 2 == 0 else (xn2, "xn2")

        def p2_ln_front(c):
            xb, xk, xch = p2_xbuf(c)
            nb_, nk_ = p2_nbuf(c)
            load_chunk(srcS, srcK, c, xb, xk, xch)
            ln_stats(xb, xk, 8, sqb=nb_, sqk=nk_)
            ln_apply(xb, xk, nb_, nk_, 8)

        def p2_body(c):
            xb, xk, xch = p2_xbuf(c)
            nb_, nk_ = p2_nbuf(c)
            if c == 0:
                pb.op("v", lambda e, nb_=nb_: e.tensor_copy(
                    lstage[:].rearrange("p (k t) -> p k t", k=8),
                    nb_[:, 0:8 * TC].rearrange("p (k t) -> p k t", k=8)
                    [:, :, 1:17]),
                    rd=[nk_], wr=["lstage"])
                pb.op("v", lambda e, nb_=nb_: e.tensor_copy(
                    odd1_bf[:].rearrange("p (k t) -> p k t", k=8),
                    nb_[:, 0:8 * TC].rearrange("p (k t) -> p k t", k=8)
                    [:, :, 2:9:2]),
                    rd=[nk_], wr=["odd1"])
                pb.op("v", lambda e, nb_=nb_: e.tensor_copy(
                    odd2_bf[:].rearrange("p (k t) -> p k t", k=8),
                    nb_[:, 0:8 * TC].rearrange("p (k t) -> p k t", k=8)
                    [:, :, 10:17:2]),
                    rd=[nk_], wr=["odd2"])
                pb.op("t", lambda e: e.transpose(
                    rep_ps[0][:, :].bitcast(BF16)[:, 0:128], lstage[:],
                    iden_sb[:]),
                    rd=["lstage", "iden_sb"], wr=["rep0"])
                pb.op("v", lambda e: e.tensor_copy(
                    ltr[:], rep_ps[0][:, :].bitcast(BF16)[:, 0:128]),
                    rd=["rep0"], wr=["ltr"])
                pb.dma(lambda e: e.dma_start(out=lscr, in_=ltr[:]),
                       rd=["ltr"], wr=["lscr"], chan="p0")
                prev = None
                for i in range(8):
                    j = 2 * i if i < 4 else 2 * (i - 4) + 8
                    prev = pb.dma(lambda e, i=i, j=j: e.dma_start(
                        out=evn[i][:].rearrange("x (k p) -> x k p", k=8),
                        in_=lscr.rearrange("(k j) p -> j k p", j=16)
                        [j].unsqueeze(0)),
                        rd=["lscr"], wr=[("evn", i)], deps=[prev], chan="p0")

            for k in range(8):
                pb.op("t", lambda e, k=k, nb_=nb_: e.matmul(
                    u1_ps[:], odd1_bf[:, k * 4:(k + 1) * 4],
                    nb_[:, k * TC:(k + 1) * TC],
                    start=(k == 0), stop=(k == 7)),
                    rd=["odd1", nk_], wr=["u1_ps"])
            pb.op("v", lambda e: e.tensor_scalar(
                u1rows[:], u1_ps[:], scl_sb[0:4, 0:1], None, ALU.mult),
                rd=["u1_ps", "scl_sb"], wr=["u1rows"])
            for i in range(4):
                pb.dma(lambda e, i=i: e.dma_start(
                    out=u1r[0:1, i * TC:(i + 1) * TC], in_=u1rows[i:i + 1, :]),
                    rd=["u1rows"], wr=["u1r"], chan="lx")

            for g in range(8):
                wb_, wk_ = wslices("w1T", g * 512, 512, 8, g % 2)
                for fi in range(4):
                    f = g * 4 + fi
                    i = f // 8
                    for k in range(8):
                        pb.op("t", lambda e, fi=fi, k=k, f=f, wb_=wb_,
                              nb_=nb_: e.matmul(
                            mm_ps[f % 2][:],
                            wb_[:, k * 512 + fi * 128:k * 512 + fi * 128 + 128],
                            nb_[:, k * TC:(k + 1) * TC],
                            start=(k == 0), stop=False),
                            rd=[wk_, nk_], wr=[("mm", f % 2)])
                    pb.op("t", lambda e, f=f, i=i: e.matmul(
                        mm_ps[f % 2][:],
                        evn[i][0:1, (f % 8) * 128:(f % 8) * 128 + 128],
                        u1r[0:1, i * TC:(i + 1) * TC],
                        start=False, stop=True),
                        rd=[("evn", i), "u1r"], wr=[("mm", f % 2)])
                    pb.op("s", lambda e, f=f: e.activation(
                        h_bf[:, f * TC:(f + 1) * TC], mm_ps[f % 2][:], AF.Gelu,
                        bias=b1_sb[:, f:f + 1]),
                        rd=[("mm", f % 2), "b1_sb"], wr=[("h", f)])

        def p2_tail(c):
            xb, xk, xch = p2_xbuf(c)
            for j in range(4):
                for fk in range(8):
                    f = j * 8 + fk
                    pb.op("t", lambda e, f=f: e.matmul(
                        v_ps[:],
                        odd2_bf[:, (f % 8) * 4:(f % 8) * 4 + 4],
                        h_bf[:, f * TC:(f + 1) * TC],
                        start=(f % 8 == 0), stop=(f % 8 == 7)),
                        rd=[("h", f), "odd2"], wr=["v_ps"])
                pb.op("v", lambda e, j=j: e.tensor_scalar(
                    v_sb[:, j * TC:(j + 1) * TC], v_ps[:],
                    scl_sb[0:4, 0:1], None, ALU.mult),
                    rd=["v_ps", "scl_sb"], wr=["v_sb"])
            for i in range(4):
                pb.dma(lambda e, i=i: e.dma_start(
                    out=vr[0:1, i * 4 * TC:(i + 1) * 4 * TC],
                    in_=v_sb[i:i + 1, :]),
                    rd=["v_sb"], wr=["vr"], chan="lx")

            for o in range(8):
                i = o // 2
                wb_, wk_ = wslices("w2T", o * 128, 128, 32, o % 2)
                for k in range(32):
                    pb.op("t", lambda e, o=o, k=k, wb_=wb_: e.matmul(
                        mm_ps[o % 2][:], wb_[:, k * 128:(k + 1) * 128],
                        h_bf[:, k * TC:(k + 1) * TC],
                        start=(k == 0), stop=False),
                        rd=[wk_, ("h", k)], wr=[("mm", o % 2)])
                for j in range(4):
                    pb.op("t", lambda e, o=o, i=i, j=j: e.matmul(
                        mm_ps[o % 2][:],
                        evn[4 + i][0:1,
                                   (o % 2) * 512 + j:(o % 2) * 512 + j + 509:4],
                        vr[0:1, (i * 4 + j) * TC:(i * 4 + j + 1) * TC],
                        start=False, stop=(j == 3)),
                        rd=[("evn", 4 + i), "vr"], wr=[("mm", o % 2)])
                pb.op("v", lambda e, o=o: e.tensor_scalar(
                    oa[:, o * TC:(o + 1) * TC], mm_ps[o % 2][:],
                    b2_sb[:, o:o + 1], None, ALU.add),
                    rd=[("mm", o % 2), "b2_sb"], wr=["oa"])
            nb_, nk_ = p2_nbuf(c)
            ln_stats(oa, "oa", 8, sqb=nb_, sqk=nk_)
            ln_apply(oa, "oa", oa, "oa", 8, res=xb, res_key=xk)
            store_chunk(dstS, dstK, c, oa, "oa")

        p2_ln_front(0)
        for c in range(NCH):
            p2_body(c)
            if c + 1 < NCH:
                p2_ln_front(c + 1)
            p2_tail(c)
        pb.barrier()
        p2ctx.close()

    if 3 in phases:
        p3ctx = ExitStack()
        sb3, ps3 = mk(p3ctx)
        (srcS, srcK), (dstS, dstK) = stream_of(3)
        xinB_3 = sb3("p3_xinB", (P, 8 * TC))
        xn5b = [sb3("p3_xn5a", (P, 8 * TC)), sb3("p3_xn5b", (P, 8 * TC))]
        hhb = [sb3("p3_hha", (P, 8 * TC), BF16), sb3("p3_hhb", (P, 8 * TC), BF16)]
        pad3 = [sb3("p3_pad0", (P, TC + 3), BF16),
                sb3("p3_pad1", (P, TC + 3), BF16)]
        acc3 = [sb3("p3_acc0", (P, TC)), sb3("p3_acc1", (P, TC))]
        xst = [sb3("p3_xst0", (P, TC)), sb3("p3_xst1", (P, TC))]
        srt = [sb3("p3_srt0", (P, TC), BF16), sb3("p3_srt1", (P, TC), BF16)]
        y3_bf = sb3("p3_y", (P, 16 * TC), BF16)
        mm3 = [ps3("p3_mm2", (P, TC)), ps3("p3_mm3", (P, TC))]

        def p3_xbuf(c):
            return ((xin, "xin", "xia") if c % 2 == 0
                    else (xinB_3, "xinB", "xib"))

        def p3_front(c):
            par = c % 2
            xb, xk, xch = p3_xbuf(c)
            load_chunk(srcS, srcK, c, xb, xk, xch)
            ln_stats(xb, xk, 8)
            ln_apply(xb, xk, xn5b[par], ("xn5", par), 8)
            ln_stats(xn5b[par], ("xn5", par), 8, rms_only=True)
            for k in range(8):
                pb.op("v", lambda e, k=k, par=par: e.scalar_tensor_tensor(
                    hhb[par][:, k * TC:(k + 1) * TC],
                    xn5b[par][:, k * TC:(k + 1) * TC],
                    nw_sb[:, k:k + 1], rep_ps[1][:], ALU.mult, ALU.mult),
                    rd=[("xn5", par), "nw_sb", "rep1"], wr=[("hh", par)])

        def p3_body(c):
            par = c % 2
            hh = hhb[par]
            hk = ("hh", par)
            for g in range(4):
                wslices("winL1", g * 512, 512, 8, 0)
                wslices("wresL1", g * 512, 512, 8, 1)
                for fi in range(4):
                    f = g * 4 + fi
                    fp = f % 2
                    for k in range(8):
                        pb.op("t", lambda e, fi=fi, k=k, hh=hh, fp=fp: e.matmul(
                            mm_ps[fp][:],
                            wsl[:, k * 512 + fi * 128:k * 512 + fi * 128 + 128],
                            hh[:, k * TC:(k + 1) * TC],
                            start=(k == 0), stop=(k == 7)),
                            rd=["wsla", hk], wr=[("mm", fp)])
                    for k in range(8):
                        pb.op("t", lambda e, fi=fi, k=k, hh=hh, fp=fp: e.matmul(
                            mm3[fp][:],
                            wslB[:, k * 512 + fi * 128:k * 512 + fi * 128 + 128],
                            hh[:, k * TC:(k + 1) * TC],
                            start=(k == 0), stop=(k == 7)),
                            rd=["wslb", hk], wr=[("mm3", fp)])
                    pb.op("s", lambda e, fp=fp: e.activation(
                        pad3[fp][:, 3:3 + TC], mm_ps[fp][:], AF.Copy),
                        rd=[("mm", fp)], wr=[("pad3", fp)])
                    pb.op("v", lambda e, f=f, fp=fp: e.tensor_copy(
                        pad3[fp][:, 0:3], halo1[:, f * 3:f * 3 + 3]),
                        rd=["halo1"], wr=[("pad3", fp)])
                    pb.op("v", lambda e, f=f, fp=fp: e.tensor_scalar(
                        acc3[fp][:], pad3[fp][:, 0:TC],
                        cw1_sb[:, 4 * f:4 * f + 1], None, ALU.mult),
                        rd=[("pad3", fp), "cw1_sb"], wr=[("acc3", fp)])
                    for kk in range(1, 4):
                        pb.op("v", lambda e, f=f, kk=kk, fp=fp:
                              e.scalar_tensor_tensor(
                            acc3[fp][:], pad3[fp][:, kk:kk + TC],
                            cw1_sb[:, 4 * f + kk:4 * f + kk + 1],
                            acc3[fp][:], ALU.mult, ALU.add),
                            rd=[("pad3", fp), ("acc3", fp), "cw1_sb"],
                            wr=[("acc3", fp)])
                    pb.op("v", lambda e, f=f, fp=fp: e.tensor_copy(
                        halo1[:, f * 3:f * 3 + 3], pad3[fp][:, TC:TC + 3]),
                        rd=[("pad3", fp)], wr=["halo1"])
                    pb.op("s", lambda e, f=f, fp=fp: e.activation(
                        xst[fp][:], acc3[fp][:], AF.Silu,
                        bias=cb1_sb[:, f:f + 1]),
                        rd=[("acc3", fp), "cb1_sb"], wr=[("xst", fp)])
                    pb.op("s", lambda e, fp=fp: e.activation(
                        srt[fp][:], mm3[fp][:], AF.Silu),
                        rd=[("mm3", fp)], wr=[("srt", fp)])
                    pb.op("v", lambda e, f=f, fp=fp: e.tensor_tensor(
                        y3_bf[:, f * TC:(f + 1) * TC], xst[fp][:], srt[fp][:],
                        ALU.mult),
                        rd=[("xst", fp), ("srt", fp)], wr=["y3_bf"])

        def p3_tail(c):
            par = c % 2
            xb, xk, xch = p3_xbuf(c)
            for o in range(8):
                wb_, wk_ = wslices("woutL1", o * 128, 128, 16, o % 2)
                for k in range(16):
                    pb.op("t", lambda e, o=o, k=k, wb_=wb_: e.matmul(
                        mm_ps[o % 2][:], wb_[:, k * 128:(k + 1) * 128],
                        y3_bf[:, k * TC:(k + 1) * TC],
                        start=(k == 0), stop=(k == 15)),
                        rd=[wk_, "y3_bf"], wr=[("mm", o % 2)])
                pb.op("v", lambda e, o=o, par=par: e.tensor_tensor(
                    oa[:, o * TC:(o + 1) * TC], mm_ps[o % 2][:],
                    xn5b[par][:, o * TC:(o + 1) * TC], ALU.add),
                    rd=[("mm", o % 2), ("xn5", par)], wr=["oa"])
            ln_stats(oa, "oa", 8)
            ln_apply(oa, "oa", oa, "oa", 8, res=xb, res_key=xk)
            store_chunk(dstS, dstK, c, oa, "oa")

        p3_front(0)
        for c in range(NCH):
            p3_body(c)
            if c + 1 < NCH:
                p3_front(c + 1)
            p3_tail(c)
        pb.barrier()
        p3ctx.close()

    if 4 in phases:
        p4ctx = ExitStack()
        sb4, ps4 = mk(p4ctx)
        (srcS, srcK), (dstS, dstK) = stream_of(4)
        xinB_4 = sb4("p4_xinB", (P, 8 * TC))
        xn4 = sb4("p4_xn2", (P, 8 * TC), BF16)
        h2a = sb4("p4_h2a", (P, 32 * TC), BF16)
        h2b = sb4("p4_h2b", (P, 32 * TC), BF16)
        mem = sb4("p4_mem", (P, 8 * TC))
        s4b = sb4("p4_s", (P, 8 * TC), BF16)
        mm4 = [ps4("p4_mm2", (P, TC)), ps4("p4_mm3", (P, TC))]
        acc4 = sb4("p4_acc", (P, 8 * TC), BF16)

        def p4_xbuf(c):
            return ((xin, "xin", "xia") if c % 2 == 0
                    else (xinB_4, "xinB", "xib"))

        def p4_nbuf(c):
            return (xn_bf, "xnsq") if c % 2 == 0 else (xn4, "xn2")

        def p4_front(c):
            xb, xk, xch = p4_xbuf(c)
            nb_, nk_ = p4_nbuf(c)
            load_chunk(srcS, srcK, c, xb, xk, xch)
            ln_stats(xb, xk, 8, sqb=nb_, sqk=nk_)
            ln_apply(xb, xk, nb_, nk_, 8)

        def p4_fc(c):
            nb_, nk_ = p4_nbuf(c)
            h2 = h2a if c % 2 == 0 else h2b
            hk = ("h2", c % 2)
            for g in range(8):
                wb_, wk_ = wslices("wfcT", g * 512, 512, 8, g % 2)
                for fi in range(4):
                    f = g * 4 + fi
                    for k in range(8):
                        pb.op("t", lambda e, fi=fi, k=k, f=f, wb_=wb_,
                              nb_=nb_: e.matmul(
                            mm_ps[f % 2][:],
                            wb_[:, k * 512 + fi * 128:k * 512 + fi * 128 + 128],
                            nb_[:, k * TC:(k + 1) * TC],
                            start=(k == 0), stop=(k == 7)),
                            rd=[wk_, nk_], wr=[("mm", f % 2)])
                    pb.op("s", lambda e, f=f: e.activation(
                        tmpf[:], mm_ps[f % 2][:], AF.Relu),
                        rd=[("mm", f % 2)], wr=["tmpf"])
                    pb.op("s", lambda e, f=f, h2=h2: e.activation(
                        h2[:, f * TC:(f + 1) * TC], tmpf[:], AF.Square),
                        rd=["tmpf"], wr=[hk])

        def p4_tail(c):
            xb, xk, xch = p4_xbuf(c)
            h2 = h2a if c % 2 == 0 else h2b
            hk = ("h2", c % 2)
            for o in range(8):
                wb_, wk_ = wslices("wprT", o * 128, 128, 32, o % 2)
                for k in range(32):
                    pb.op("t", lambda e, o=o, k=k, wb_=wb_, h2=h2: e.matmul(
                        mm4[o % 2][:], wb_[:, k * 128:(k + 1) * 128],
                        h2[:, k * TC:(k + 1) * TC],
                        start=(k == 0), stop=(k == 31)),
                        rd=[wk_, hk], wr=[("mm4", o % 2)])
                pb.op("s", lambda e, o=o: e.activation(
                    oa[:, o * TC:(o + 1) * TC], mm4[o % 2][:], AF.Copy),
                    rd=[("mm4", o % 2)], wr=["oa"])
            W = 8 * TC
            pb.op("v", lambda e: e.tensor_copy(mem[:, 0:W], oa[:, 0:W]),
                  rd=["oa"], wr=["mem"])
            pb.op("v", lambda e: e.tensor_single_scalar(
                acc4[:, 0:W], mem[:, 0:W], 1.0, ALU.is_gt),
                rd=["mem"], wr=["acc4"])
            pb.op("v", lambda e: e.tensor_copy(s4b[:, 0:W], acc4[:, 0:W]),
                  rd=["acc4"], wr=["s4"])
            for k in range(4):
                pb.op("v", lambda e: e.scalar_tensor_tensor(
                    mem[:, 0:W], mem[:, 0:W], 0.9, oa[:, 0:W],
                    ALU.mult, ALU.add),
                    rd=["mem", "oa"], wr=["mem"])
                pb.op("v", lambda e: e.tensor_tensor(
                    mem[:, 0:W], mem[:, 0:W], s4b[:, 0:W], ALU.subtract),
                    rd=["mem", "s4"], wr=["mem"])
                pb.op("v", lambda e: e.tensor_single_scalar(
                    s4b[:, 0:W], mem[:, 0:W], 1.0, ALU.is_gt),
                    rd=["mem"], wr=["s4"])
                pb.op("v", lambda e: e.tensor_tensor(
                    acc4[:, 0:W], acc4[:, 0:W], s4b[:, 0:W], ALU.add),
                    rd=["acc4", "s4"], wr=["acc4"])
            pb.op("v", lambda e: e.tensor_single_scalar(
                oa[:, 0:W], acc4[:, 0:W], 0.2, ALU.mult),
                rd=["acc4"], wr=["oa"])
            nb_, nk_ = p4_nbuf(c)
            ln_stats(oa, "oa", 8, sqb=nb_, sqk=nk_)
            ln_apply(oa, "oa", oa, "oa", 8, res=xb, res_key=xk)
            store_chunk(dstS, dstK, c, oa, "oa")

        p4_front(0)
        p4_fc(0)
        for c in range(NCH):
            if c + 1 < NCH:
                p4_front(c + 1)
                p4_fc(c + 1)
            p4_tail(c)
        pb.barrier()
        p4ctx.close()

    pb.barrier()
    pb.finalize()

    with ExitStack() as sctx:
        sems = {}
        for name in list(ENGS) + sorted(pb.chans):
            sems[name] = sctx.enter_context(nc.semaphore("sem_" + str(name)))
        with nc.Block() as block:
            @block.sync
            def _(e):
                pb.replay("q", e, sems)
                for chname in sorted(pb.chans):
                    tot = pb.totals.get(chname, 0)
                    if tot:
                        e.wait_ge(sems[chname], tot)

            @block.tensor
            def _(e):
                pb.replay("t", e, sems)

            @block.vector
            def _(e):
                pb.replay("v", e, sems)

            @block.scalar
            def _(e):
                pb.replay("s", e, sems)

            @block.gpsimd
            def _(e):
                pb.replay("g", e, sems)

    ctx.close()
    return nc


def prep_inputs(inputs):
    T = lambda a: np.ascontiguousarray(np.asarray(a).T)

    def pmajor(v):
        v = np.asarray(v)
        tail = v.shape[1:]
        k = v.shape[0] // P
        return np.ascontiguousarray(
            v.reshape((k, P) + tail).swapaxes(0, 1).reshape(P, -1))

    common = {
        "wioT": T(inputs["L0_attn_W_in"]),
        "wxT": pmajor(np.asarray(inputs["L0_attn_W_x"]).T),
        "cw0": pmajor(np.asarray(inputs["L0_attn_conv_w"])[:, 0, :]),
        "cb0": pmajor(inputs["L0_attn_conv_b"]),
        "wdt": pmajor(inputs["L0_attn_w_dt"]),
        "bdt": pmajor(inputs["L0_attn_b_dt"]),
        "alog": pmajor(inputs["L0_attn_A_log"]),
        "dpar": pmajor(inputs["L0_attn_D_param"]),
        "woT": T(inputs["L0_attn_W_out"]),
        "w1T": T(inputs["L0_ffn_W1"]),
        "b1": pmajor(inputs["L0_ffn_b1"]),
        "w2T": T(inputs["L0_ffn_W2"]),
        "b2": pmajor(inputs["L0_ffn_b2"]),
        "scl0": np.asarray(inputs["L0_ffn_scale"], np.float32).reshape(1, 1),
        "nw1": pmajor(inputs["L1_attn_norm_w"]),
        "winL1": np.ascontiguousarray(inputs["L1_attn_w_in"]),
        "wresL1": np.ascontiguousarray(inputs["L1_attn_w_res"]),
        "cw1": pmajor(inputs["L1_attn_conv_w"]),
        "cb1": pmajor(inputs["L1_attn_conv_b"]),
        "woutL1": np.ascontiguousarray(inputs["L1_attn_w_out"]),
        "wfcT": T(inputs["L1_ffn_W_fc"]),
        "wprT": T(inputs["L1_ffn_W_proj"]),
    }
    sel = np.zeros((33, 33 * 128), dtype=BF)
    for n in range(33):
        sel[n, n * 128:(n + 1) * 128] = 1
    common["sel33"] = sel
    common["ones1"] = np.ones((1, 128), dtype=BF)
    common["onesc"] = np.ones((128, 1), dtype=BF)
    common["iden"] = np.eye(128, dtype=np.float32).astype(BF)
    x = np.asarray(inputs["L0_attn_x"])
    in_maps = []
    for b in range(8):
        m = dict(common)
        m["xT"] = np.ascontiguousarray(x[b].T)
        in_maps.append(m)
    return in_maps


def kernel(**inputs):
    in_maps = prep_inputs(inputs)
    nc = build_program(phases=(1, 2, 3, 4))
    res = run_bass_kernel_spmd(nc, in_maps, core_ids=list(range(8)))
    out = np.stack([np.asarray(r["out"]).T for r in res.results])
    return np.ascontiguousarray(out.astype(np.float32))


# version 26
# speedup vs baseline: 1.0510x; 1.0510x over previous
import numpy as np
import ml_dtypes
from contextlib import ExitStack

import concourse.bass as bass
import concourse.mybir as mybir
from concourse.bass_utils import run_bass_kernel_spmd

F32 = mybir.dt.float32
BF16 = mybir.dt.bfloat16
AF = mybir.ActivationFunctionType
ALU = mybir.AluOpType
BF = ml_dtypes.bfloat16

P = 128
TC = 512
L = 4096
NCH = L // TC
NG = 4
SC = TC + 1
EPS = 1e-5
ENGS = ("q", "t", "v", "s", "g")
LN_ENG = "v"
INPLACE_SP = True


class PB:

    def __init__(self):
        self.ops = []
        self.wmap = {}
        self.rmap = {}
        self.last_eng = {}
        self.last_chan = {}
        self.pending = {}
        self.chans = set()

    def op(self, eng, fn, rd=(), wr=(), deps=(), force=(), chan=None):
        rec = {"eng": eng, "fn": fn, "chan": chan or eng,
               "dma": chan is not None, "idx": len(self.ops),
               "deps": [x for x in deps if x is not None],
               "force": [x for x in force if x is not None],
               "need_inc": False, "semval": None}
        if chan is not None:
            self.chans.add(chan)
        for k in rd:
            w = self.wmap.get(k)
            if w is not None:
                rec["deps"].append(w)
            self.rmap.setdefault(k, []).append(rec)
        for k in wr:
            w = self.wmap.get(k)
            if w is not None:
                rec["deps"].append(w)
            rec["deps"].extend(self.rmap.get(k, ()))
            self.wmap[k] = rec
            self.rmap[k] = []
        bd = self.pending.pop(eng, None)
        if bd:
            rec["deps"].extend(bd)
        rec["deps"] = [x for x in rec["deps"] if x is not rec]
        self.ops.append(rec)
        self.last_eng[eng] = rec
        if chan is not None:
            self.last_chan[chan] = rec
        return rec

    def dma(self, fn, rd=(), wr=(), deps=(), chan="dm", eng="q"):
        return self.op(eng, fn, rd, wr, deps, chan=chan)

    def barrier(self):
        snap = list(self.last_eng.values()) + list(self.last_chan.values())
        for e in ENGS:
            self.pending[e] = list(snap)

    def finalize(self):
        for op in self.ops:
            for x in op["deps"]:
                if x["dma"] or x["eng"] != op["eng"]:
                    x["need_inc"] = True
            for x in op["force"]:
                x["need_inc"] = True
            if op["dma"]:
                op["need_inc"] = True
        cnt = {}
        for op in self.ops:
            if op["need_inc"]:
                cnt[op["chan"]] = cnt.get(op["chan"], 0) + (16 if op["dma"] else 1)
                op["semval"] = cnt[op["chan"]]
        self.totals = cnt

    def replay(self, eng, e, sems):
        waited = {}
        for op in self.ops:
            if op["eng"] != eng:
                continue
            forced = set(id(x) for x in op["force"])
            for x in op["deps"] + op["force"]:
                if (x["eng"] == op["eng"] and not x["dma"]
                        and id(x) not in forced):
                    continue
                val = x["semval"]
                if val is None:
                    continue
                if waited.get(x["chan"], 0) < val:
                    e.wait_ge(sems[x["chan"]], val)
                    waited[x["chan"]] = val
            inst = op["fn"](e)
            if op["need_inc"]:
                inst.then_inc(sems[op["chan"]], 16 if op["dma"] else 1)


def build_program(phases=(1, 2, 3, 4), dbg_streams=False):
    nc = bass.Bass("TRN2", target_bir_lowering=False)
    d = {}

    def din(name, shape, dt=F32):
        d[name] = nc.dram_tensor(name, list(shape), dt, kind="ExternalInput").ap()
        return d[name]

    for nm, shape, dt in [
            ("xT", (1024, L), F32), ("wioT", (1024, 4096), F32),
            ("wxT", (128, 528), F32), ("cw0", (128, 64), F32),
            ("cb0", (128, 16), F32), ("wdt", (128, 16), F32),
            ("bdt", (128, 16), F32), ("alog", (128, 256), F32),
            ("dpar", (128, 16), F32), ("woT", (2048, 1024), F32),
            ("w1T", (1024, 4096), F32), ("b1", (128, 32), F32),
            ("w2T", (4096, 1024), F32), ("b2", (128, 8), F32),
            ("scl0", (1, 1), F32), ("nw1", (128, 8), F32),
            ("winL1", (1024, 2048), F32), ("wresL1", (1024, 2048), F32),
            ("cw1", (128, 64), F32), ("cb1", (128, 16), F32),
            ("woutL1", (2048, 1024), F32), ("wfcT", (1024, 4096), F32),
            ("wprT", (4096, 1024), F32), ("sel33", (33, 33 * 128), BF16),
            ("ones1", (1, 128), BF16), ("onesc", (128, 1), BF16),
            ("iden", (128, 128), BF16)]:
        din(nm, shape, dt)

    out_ap = nc.dram_tensor("out", [1024, L], F32, kind="ExternalOutput").ap()
    skind = "ExternalOutput" if dbg_streams else "Internal"
    X = {1: nc.dram_tensor("X1", [1024, L], F32, kind=skind).ap(),
         2: nc.dram_tensor("X2", [1024, L], F32, kind=skind).ap(),
         3: nc.dram_tensor("X3", [1024, L], F32, kind=skind).ap()}
    WS = {}
    WGEO = {"wioT": (512, 8), "woT": (128, 16), "w1T": (512, 8),
            "w2T": (128, 32), "winL1": (512, 8), "wresL1": (512, 8),
            "woutL1": (128, 16), "wfcT": (512, 8), "wprT": (128, 32)}
    for nm, shape in [("wioT", (1024, 4096)), ("woT", (2048, 1024)),
                      ("w1T", (1024, 4096)), ("w2T", (4096, 1024)),
                      ("winL1", (1024, 2048)), ("wresL1", (1024, 2048)),
                      ("woutL1", (2048, 1024)), ("wfcT", (1024, 4096)),
                      ("wprT", (4096, 1024))]:
        WS[nm] = nc.dram_tensor("ws_" + nm, [P, shape[0] * shape[1] // P],
                                BF16, kind="Internal").ap()
    lscr = nc.dram_tensor("lscr", [128, 128], BF16, kind="Internal").ap()

    pb = PB()
    ctx = ExitStack()

    def mk(c):
        def sbf(name, shape, dt=F32):
            return c.enter_context(nc.sbuf_tensor(name, list(shape), dt))

        def psf(name, shape, dt=F32):
            return c.enter_context(nc.psum_tensor(name, list(shape), dt))
        return sbf, psf

    sb, ps_ = mk(ctx)

    cw0_sb = sb("cw0_sb", (P, 64))
    cb0_sb = sb("cb0_sb", (P, 16))
    wdt_sb = sb("wdt_sb", (P, 16))
    bdt_sb = sb("bdt_sb", (P, 16))
    dpar_sb = sb("dpar_sb", (P, 16))
    a_sb = sb("a_sb", (P, 256))
    cw1_sb = sb("cw1_sb", (P, 64))
    cb1_sb = sb("cb1_sb", (P, 16))
    nw_sb = sb("nw_sb", (P, 8))
    b1_sb = sb("b1_sb", (P, 32))
    b2_sb = sb("b2_sb", (P, 8))
    sel_sb = sb("sel_sb", (33, 33 * 128), BF16)
    ones_sb = sb("ones_sb", (1, 128), BF16)
    onesc_sb = sb("onesc_sb", (P, 1), BF16)
    onesc32 = sb("onesc32", (P, 1))
    iden_sb = sb("iden_sb", (P, 128), BF16)
    sclrow = sb("sclrow", (1, 1))
    scl_sb = sb("scl_sb", (P, 1))
    hstate = sb("hstate", (P, 256))
    halo0 = sb("halo0", (P, 48), BF16)
    halo1 = sb("halo1", (P, 48), BF16)
    cnst = sb("cnst", (P, 3))
    wxT_sb = sb("wxT_sb", (P, 16 * 33), BF16)

    xin = sb("xin", (P, 8 * TC))
    x_bf = sb("x_bf", (P, 8 * TC), BF16)
    xn_bf = sb("xnsq", (P, 8 * TC), BF16)
    xsq_bf = xn_bf
    oa = sb("oa", (P, 8 * TC))
    wsl = sb("wsl", (P, 4096), BF16)
    wslB = sb("wslB", (P, 4096), BF16)
    tmpf = sb("tmpf", (P, TC))
    mrow = sb("mrow", (1, TC))
    vrow = sb("vrow", (1, TC))
    rrow = sb("rrow", (1, TC))
    m_bfr = sb("m_bfr", (1, TC), BF16)
    r_bfr = sb("r_bfr", (1, TC), BF16)

    s_ps = ps_("s_ps", (1, TC))
    sq_ps = ps_("sq_ps", (1, TC))
    rep_ps = [ps_("rep_ps0", (P, TC)), ps_("rep_ps1", (P, TC))]
    mm_ps = [ps_("mm_ps0", (P, TC)), ps_("mm_ps1", (P, TC))]

    KP = "params"

    with ExitStack() as p0ctx:
        sb0, _ = mk(p0ctx)
        pb.dma(lambda e: e.dma_start(out=cw0_sb[:], in_=d["cw0"]),
               wr=[KP, "cw0_sb"], chan="p0")
        pb.dma(lambda e: e.dma_start(out=cw1_sb[:], in_=d["cw1"]),
               wr=[KP, "cw1_sb"], chan="p0")
        pb.dma(lambda e: e.dma_start(out=a_sb[:], in_=d["alog"]),
               wr=[KP, "a_sb"], chan="p0")
        for dst, nm, key in [(cb0_sb, "cb0", "cb0_sb"), (wdt_sb, "wdt", "wdt_sb"),
                             (bdt_sb, "bdt", "bdt_sb"), (dpar_sb, "dpar", "dpar_sb"),
                             (cb1_sb, "cb1", "cb1_sb"), (nw_sb, "nw1", "nw_sb"),
                             (b1_sb, "b1", "b1_sb"), (b2_sb, "b2", "b2_sb")]:
            pb.dma(lambda e, d0=dst, s0=d[nm]: e.dma_start(
                out=d0[:], in_=s0), wr=[KP, key], chan="p0")
        pb.dma(lambda e: e.dma_start(out=sel_sb[:], in_=d["sel33"]),
               wr=[KP, "sel_sb"], chan="p0")
        pb.dma(lambda e: e.dma_start(out=ones_sb[:], in_=d["ones1"]),
               wr=[KP, "ones_sb"], chan="p0")
        pb.dma(lambda e: e.dma_start(out=onesc_sb[:], in_=d["onesc"]),
               wr=[KP, "onesc_sb"], chan="p0")
        pb.op("v", lambda e: e.memset(onesc32[:], 1.0), wr=["onesc32"])
        pb.dma(lambda e: e.dma_start(out=iden_sb[:], in_=d["iden"]),
               wr=[KP, "iden_sb"], chan="p0")
        pb.dma(lambda e: e.dma_start(out=sclrow[:], in_=d["scl0"]),
               wr=[KP, "sclrow"], chan="p0")
        pb.op("s", lambda e: e.activation(a_sb[:], a_sb[:], AF.Exp),
              rd=["a_sb"], wr=["a_sb"])
        pb.op("v", lambda e: e.tensor_scalar_mul(a_sb[:], a_sb[:], -1.0),
              rd=["a_sb"], wr=["a_sb"])
        pb.dma(lambda e: e.dma_start(out=scl_sb[0:1, :], in_=sclrow[:]),
               rd=["sclrow"], wr=["scl_sb"], chan="p0")
        width = 1
        while width < P:
            w2 = min(width, P - width)
            pb.dma(lambda e, a=width, b=w2: e.dma_start(
                out=scl_sb[a:a + b, :], in_=scl_sb[0:b, :]),
                rd=["scl_sb"], wr=["scl_sb"], chan="p0")
            width += w2
        pb.op("v", lambda e: e.memset(cnst[:, 0:1], EPS), wr=["cnst"])
        pb.op("v", lambda e: e.memset(cnst[:, 1:2], -1.0), wr=["cnst"])
        pb.op("v", lambda e: e.memset(cnst[:, 2:3], -0.5), wr=["cnst"])
        pb.op("v", lambda e: e.memset(hstate[:], 0.0), wr=["hstate"])
        pb.op("v", lambda e: e.memset(halo0[:], 0.0), wr=["halo0"])
        pb.op("v", lambda e: e.memset(halo1[:], 0.0), wr=["halo1"])

        wxstage = sb0("wxstage", (P, 16 * 33))
        pb.dma(lambda e: e.dma_start(out=wxstage[:], in_=d["wxT"]),
            wr=["wxstage"], chan="p0")
        pb.op("v", lambda e: e.tensor_copy(wxT_sb[:], wxstage[:]),
              rd=["wxstage"], wr=["wxT_sb"])

        need = {1: ["wioT", "woT"], 2: ["w1T", "w2T"],
                3: ["winL1", "wresL1", "woutL1"], 4: ["wfcT", "wprT"]}
        cast_list = []
        for ph in phases:
            cast_list.extend(need[ph])
        jobs = []
        for nm in cast_list:
            srcT, dst = d[nm], WS[nm]
            rows, cols = srcT.tensor.shape
            for r0 in range(0, rows, P):
                for c0 in range(0, cols, 2048):
                    cw = min(2048, cols - c0)
                    jobs.append((srcT, dst, nm, r0, c0, cw))
        for ji, (srcT, dst, nm, r0, c0, cw) in enumerate(jobs):
            pr = ji % 2
            ob = pr * 2048
            gc, nk = WGEO[nm]
            kk = r0 // P
            ng_ = cw // gc
            base = (c0 // gc) * (nk * gc) + kk * gc
            pb.dma(lambda e, s=srcT, r=r0, c0=c0, cw=cw, ob=ob: e.dma_start(
                out=oa[:, ob:ob + cw], in_=s[r:r + P, c0:c0 + cw]),
                wr=[("oa", "c", pr)], chan="wca" if pr == 0 else "wcb",
                eng="s" if pr == 0 else "g")
            pb.op("v", lambda e, cw=cw, ob=ob: e.tensor_copy(
                x_bf[:, ob:ob + cw], oa[:, ob:ob + cw]),
                rd=[("oa", "c", pr)], wr=[("xbf", "c", pr)])
            pb.dma(lambda e, ds=dst, g0=c0 // gc, kk=kk, ng_=ng_, gc=gc,
                   nk=nk, cw=cw, ob=ob: e.dma_start(
                out=ds.rearrange("p (g k c) -> p g k c", k=nk, c=gc)
                [:, g0:g0 + ng_, kk:kk + 1, :],
                in_=x_bf[:, ob:ob + cw]
                .rearrange("p (g x c) -> p g x c", x=1, c=gc)),
                rd=[("xbf", "c", pr)], wr=[("ws", nm)],
                chan="wcc" if pr == 0 else "wcd")
        pb.op("v", lambda e: e.memset(oa[:, 0:1], 0.0),
              rd=[("oa", "c", 0), ("oa", "c", 1),
                  ("xbf", "c", 0), ("xbf", "c", 1)],
              wr=["oa"])

    def ln_stats(src, src_key, ntiles, rms_only=False, sqb=None, sqk=None):
        nt = ntiles * TC
        if sqb is None:
            sqb, sqk = xsq_bf, "xnsq"
        pb.op("s", lambda e, sqb=sqb: e.activation(
            sqb[:, 0:nt], src[:, 0:nt], AF.Square),
              rd=[src_key], wr=[sqk])
        for k in range(ntiles):
            if not rms_only:
                pb.op("t", lambda e, k=k: e.matmul(
                    s_ps[:], onesc32[:], src[:, k * TC:(k + 1) * TC],
                    start=(k == 0), stop=(k == ntiles - 1)),
                    rd=[src_key, "onesc32"], wr=["s_ps"])
            pb.op("t", lambda e, k=k, sqb=sqb: e.matmul(
                sq_ps[:], onesc_sb[:], sqb[:, k * TC:(k + 1) * TC],
                start=(k == 0), stop=(k == ntiles - 1)),
                rd=[sqk, "onesc_sb"], wr=["sq_ps"])
        dd = float(ntiles * P)
        if rms_only:
            pb.op("v", lambda e: e.tensor_single_scalar(
                vrow[:], sq_ps[:], 1.0 / dd, ALU.mult),
                rd=["sq_ps"], wr=["vrow"])
        else:
            pb.op("v", lambda e: e.tensor_single_scalar(
                mrow[:], s_ps[:], 1.0 / dd, ALU.mult),
                rd=["s_ps"], wr=["mrow"])
            pb.op("v", lambda e: e.tensor_single_scalar(
                vrow[:], sq_ps[:], 1.0 / dd, ALU.mult),
                rd=["sq_ps"], wr=["vrow"])
            pb.op("v", lambda e: e.tensor_tensor(
                rrow[:], mrow[:], mrow[:], ALU.mult),
                rd=["mrow"], wr=["rrow"])
            pb.op("v", lambda e: e.tensor_tensor(
                vrow[:], vrow[:], rrow[:], ALU.subtract),
                rd=["vrow", "rrow"], wr=["vrow"])
            pb.op("v", lambda e: e.tensor_copy(m_bfr[:], mrow[:]),
                  rd=["mrow"], wr=["m_bfr"])
        pb.op("s", lambda e: e.activation(vrow[:], vrow[:], AF.Ln,
                                          bias=cnst[0:1, 0:1]),
              rd=["vrow", "cnst"], wr=["vrow"])
        pb.op("s", lambda e: e.activation(rrow[:], vrow[:], AF.Exp,
                                          scale=cnst[0:1, 2:3]),
              rd=["vrow", "cnst"], wr=["rrow"])
        pb.op("v", lambda e: e.tensor_copy(r_bfr[:], rrow[:]),
              rd=["rrow"], wr=["r_bfr"])
        if not rms_only:
            pb.op("t", lambda e: e.matmul(rep_ps[0][:], ones_sb[:], m_bfr[:],
                                          start=True, stop=True),
                  rd=["m_bfr", "ones_sb"], wr=["rep0"])
        pb.op("t", lambda e: e.matmul(rep_ps[1][:], ones_sb[:], r_bfr[:],
                                      start=True, stop=True),
              rd=["r_bfr", "ones_sb"], wr=["rep1"])

    def ln_apply(src, src_key, dst, dst_key, ntiles, res=None, res_key=None):
        for k in range(ntiles):
            pb.op("v", lambda e, k=k: e.tensor_tensor(
                tmpf[:], src[:, k * TC:(k + 1) * TC], rep_ps[0][:],
                ALU.subtract), rd=[src_key, "rep0"], wr=["tmpf"])
            if res is None:
                pb.op("v", lambda e, k=k: e.tensor_tensor(
                    dst[:, k * TC:(k + 1) * TC], tmpf[:], rep_ps[1][:],
                    ALU.mult), rd=["tmpf", "rep1"], wr=[dst_key])
            else:
                pb.op("v", lambda e, k=k: e.tensor_tensor(
                    tmpf[:], tmpf[:], rep_ps[1][:], ALU.mult),
                    rd=["tmpf", "rep1"], wr=["tmpf"])
                pb.op("v", lambda e, k=k: e.tensor_tensor(
                    dst[:, k * TC:(k + 1) * TC], tmpf[:],
                    res[:, k * TC:(k + 1) * TC], ALU.add),
                    rd=["tmpf", res_key], wr=[dst_key])

    def load_chunk(src_stream, skey, c, buf=None, bkey="xin", chan="xia"):
        if buf is None:
            buf = xin
        pb.dma(lambda e, t0=c * TC, b=buf: e.dma_start(
            out=b[:].rearrange("p (k t) -> p k t", k=8),
            in_=src_stream[:, t0:t0 + TC].rearrange("(k p) t -> p k t", p=P)),
            rd=[skey], wr=[bkey], chan=chan)

    def store_chunk(dst_stream, dkey, c, src, src_key):
        pb.dma(lambda e, t0=c * TC: e.dma_start(
            out=dst_stream[:, t0:t0 + TC].rearrange("(k p) t -> p k t", p=P),
            in_=src[:].rearrange("p (k t) -> p k t", k=8)),
            rd=[src_key], wr=[dkey], chan="st")

    WB = [(None, "wsla", "wa"), (None, "wslb", "wb")]

    def wslices(ws_name, col0, ncols, nk, par):
        buf, bkey, chan = WB[par]
        g = col0 // ncols
        pb.dma(lambda e, b=buf, g=g, w=nk * ncols: e.dma_start(
            out=b[:, 0:w], in_=WS[ws_name][:, g * w:(g + 1) * w]),
            rd=[("ws", ws_name)], wr=[bkey], chan=chan,
            eng="q" if par == 0 else "g")
        return buf, bkey

    WB[0] = (wsl, "wsla", "wa")
    WB[1] = (wslB, "wslb", "wb")

    active = [ph for ph in (1, 2, 3, 4) if ph in phases]

    def stream_of(ph):
        i = active.index(ph)
        src = (d["xT"], "xT") if i == 0 else (X[active[i - 1]], ("X", active[i - 1]))
        dst = (out_ap, "OUT") if i == len(active) - 1 else (X[ph], ("X", ph))
        return src, dst


    if 1 in phases:
        p1ctx = ExitStack()
        sb1, ps1 = mk(p1ctx)
        (srcS, srcK), (dstS, dstK) = stream_of(1)
        xc_bf = sb1("p1_xc", (P, 16 * TC), BF16)
        sz_bf = sb1("p1_sz", (P, 16 * TC), BF16)
        pad0 = sb1("p1_pad0", (P, TC + 3), BF16)
        pad = [pad0, pad0]
        accv = [tmpf, tmpf]
        xp_bf = sb1("p1_xp", (33, TC), BF16)
        brep = sb1("p1_brep", (P, 16 * TC), BF16)
        crep = sb1("p1_crep", (P, 16 * TC), BF16)
        xinB_1 = sb1("p1_xinB", (P, 8 * TC))
        dap0 = sb1("p1_dap", (P, NG * SC))
        dx0 = sb1("p1_dx", (P, NG * SC))
        dapA = dapB = dap0
        dxA = dxB = dx0
        spx = tmpf
        spt = sb1("p1_spt", (P, TC))
        delta = spt
        uu = sb1("p1_u", (P, TC))
        y_acc = sb1("p1_yacc", (P, TC))
        y_g = sb1("p1_yg", (P, 16 * TC), BF16)
        xp_ps = ps1("p1_xp_ps", (33, TC))
        dtx_ps = ps1("p1_dtx_ps", (P, TC))

        pb.op("v", lambda e: e.memset(dapA[:], 0.0), wr=[("dap", 0)])
        pb.op("v", lambda e: e.memset(dapB[:], 0.0), wr=[("dap", 1)])
        pb.op("v", lambda e: e.memset(dxA[:], 0.0), wr=[("dx", 0)])
        pb.op("v", lambda e: e.memset(dxB[:], 0.0), wr=[("dx", 1)])

        def emit_trees(job):
            par, dxq, ng = job
            pb.op("v", lambda e, dxq=dxq: e.tensor_tensor(
                dxq[:, 1:1 + TC], dxq[:, 1:1 + TC],
                dxq[:, 2 * SC + 1:2 * SC + 1 + TC], ALU.add),
                rd=[("dx", par)], wr=[("dx", par)])
            pb.op("v", lambda e, dxq=dxq: e.tensor_tensor(
                dxq[:, 1:1 + TC], dxq[:, 1:1 + TC],
                dxq[:, SC + 1:SC + 1 + TC], ALU.add),
                rd=[("dx", par)], wr=[("dx", par)])
            pb.op("v", lambda e, dxq=dxq: e.tensor_tensor(
                dxq[:, 1:1 + TC], dxq[:, 1:1 + TC],
                dxq[:, 3 * SC + 1:3 * SC + 1 + TC], ALU.add),
                rd=[("dx", par)], wr=[("dx", par)])
            if ng == 0:
                pb.op("v", lambda e, dxq=dxq: e.tensor_copy(
                    y_acc[:], dxq[:, 1:1 + TC]),
                    rd=[("dx", par)], wr=["y_acc"])
            else:
                pb.op("v", lambda e, dxq=dxq: e.tensor_tensor(
                    y_acc[:], y_acc[:], dxq[:, 1:1 + TC], ALU.add),
                    rd=[("dx", par), "y_acc"], wr=["y_acc"])

        def p1_xbuf(c):
            return ((xin, "xin", "xia") if c % 2 == 0
                    else (xinB_1, "xinB", "xib"))

        def p1_front(c):
            xb, xk, xch = p1_xbuf(c)
            load_chunk(srcS, srcK, c, xb, xk, xch)
            ln_stats(xb, xk, 8)
            ln_apply(xb, xk, xn_bf, "xnsq", 8)

            for g in range(8):
                wb_, wk_ = wslices("wioT", g * 512, 512, 8, g % 2)
                for fi in range(4):
                    f = g * 4 + fi
                    for k in range(8):
                        pb.op("t", lambda e, fi=fi, k=k, f=f, wb_=wb_: e.matmul(
                            mm_ps[f % 2][:],
                            wb_[:, k * 512 + fi * 128: k * 512 + fi * 128 + 128],
                            xn_bf[:, k * TC:(k + 1) * TC],
                            start=(k == 0), stop=(k == 7)),
                            rd=[wk_, "xnsq"], wr=[("mm", f % 2)])
                    if f < 16:
                        pb.op("s", lambda e, f=f: e.activation(
                            pad[f % 2][:, 3:3 + TC], mm_ps[f % 2][:], AF.Copy),
                            rd=[("mm", f % 2)], wr=[("pad", 0)])
                        pb.op("v", lambda e, f=f: e.tensor_copy(
                            pad[f % 2][:, 0:3], halo0[:, f * 3:f * 3 + 3]),
                            rd=["halo0"], wr=[("pad", 0)])
                        pb.op("v", lambda e, f=f: e.tensor_scalar(
                            accv[f % 2][:], pad[f % 2][:, 0:TC],
                            cw0_sb[:, 4 * f:4 * f + 1], None, ALU.mult),
                            rd=[("pad", 0), "cw0_sb"], wr=["tmpf"])
                        for kk in range(1, 4):
                            pb.op("v", lambda e, f=f, kk=kk:
                                  e.scalar_tensor_tensor(
                                      accv[f % 2][:], pad[f % 2][:, kk:kk + TC],
                                      cw0_sb[:, 4 * f + kk:4 * f + kk + 1],
                                      accv[f % 2][:], ALU.mult, ALU.add),
                                  rd=[("pad", 0), "tmpf", "cw0_sb"],
                                  wr=["tmpf"])
                        pb.op("v", lambda e, f=f: e.tensor_copy(
                            halo0[:, f * 3:f * 3 + 3], pad[f % 2][:, TC:TC + 3]),
                            rd=[("pad", 0)], wr=["halo0"])
                        pb.op("s", lambda e, f=f: e.activation(
                            xc_bf[:, f * TC:(f + 1) * TC], accv[f % 2][:],
                            AF.Silu, bias=cb0_sb[:, f:f + 1]),
                            rd=["tmpf", "cb0_sb"], wr=[("xc", f)])
                    else:
                        pb.op("s", lambda e, f=f: e.activation(
                            sz_bf[:, (f - 16) * TC:(f - 15) * TC],
                            mm_ps[f % 2][:], AF.Silu),
                            rd=[("mm", f % 2)], wr=["sz_bf"])

            for k in range(16):
                pb.op("t", lambda e, k=k: e.matmul(
                    xp_ps[:], wxT_sb[:, k * 33:(k + 1) * 33],
                    xc_bf[:, k * TC:(k + 1) * TC],
                    start=(k == 0), stop=(k == 15)),
                    rd=[("xc", k), "wxT_sb"], wr=["xp_ps"])
            pb.op("v", lambda e: e.tensor_copy(xp_bf[:], xp_ps[:]),
                  rd=["xp_ps"], wr=["xp_bf"])
            for n in range(16):
                pb.op("t", lambda e, n=n: e.matmul(
                    rep_ps[0][:], sel_sb[:, n * P:(n + 1) * P], xp_bf[:],
                    start=True, stop=True),
                    rd=["xp_bf", "sel_sb"], wr=["rep0"])
                pb.op("v", lambda e, n=n: e.tensor_copy(
                    brep[:, n * TC:(n + 1) * TC], rep_ps[0][:]),
                    rd=["rep0"], wr=["brep"])
                pb.op("t", lambda e, n=n: e.matmul(
                    rep_ps[1][:], sel_sb[:, (16 + n) * P:(17 + n) * P], xp_bf[:],
                    start=True, stop=True),
                    rd=["xp_bf", "sel_sb"], wr=["rep1"])
                pb.op("v", lambda e, n=n: e.tensor_copy(
                    crep[:, n * TC:(n + 1) * TC], rep_ps[1][:]),
                    rd=["rep1"], wr=["crep"])
            pb.op("t", lambda e: e.matmul(
                dtx_ps[:], sel_sb[:, 32 * P:33 * P], xp_bf[:],
                start=True, stop=True),
                rd=["xp_bf", "sel_sb"], wr=["dtx_ps"])

        def p1_scan(c):
            for dt in range(16):
                pb.op("s", lambda e, dt=dt: e.activation(
                    spx[:], dtx_ps[:], AF.Identity,
                    scale=wdt_sb[:, dt:dt + 1], bias=bdt_sb[:, dt:dt + 1]),
                    rd=["dtx_ps", "wdt_sb", "bdt_sb"], wr=["tmpf"])
                pb.op("s", lambda e: e.activation(spt[:], spx[:], AF.Abs),
                      rd=["tmpf"], wr=["spt"])
                pb.op("s", lambda e: e.activation(spt[:], spt[:], AF.Exp,
                                                  scale=cnst[:, 1:2]),
                      rd=["spt", "cnst"], wr=["spt"])
                pb.op("s", lambda e: e.activation(uu[:], spt[:], AF.Ln,
                                                  bias=1.0),
                      rd=["spt"], wr=["uu"])
                pb.op("v", lambda e: e.scalar_tensor_tensor(
                    delta[:], spx[:], 0.0, uu[:], ALU.max, ALU.add),
                    rd=["tmpf", "uu"], wr=["spt"])
                pb.op("v", lambda e, dt=dt: e.tensor_tensor(
                    uu[:], delta[:], xc_bf[:, dt * TC:(dt + 1) * TC], ALU.mult),
                    rd=["spt", ("xc", dt)], wr=["uu"])
                pending = []
                for ng in range(16 // NG):
                    nb = ng * NG
                    it = dt * (16 // NG) + ng
                    par = 0
                    dap = dapA
                    dbxp = dxA
                    acts = []
                    for nn in range(NG):
                        acts.append(pb.op("s", lambda e, dt=dt, nn=nn, nb=nb,
                                          dap=dap: e.activation(
                            dap[:, nn * SC + 1: nn * SC + 1 + TC], delta[:],
                            AF.Exp,
                            scale=a_sb[:, dt * 16 + nb + nn:
                                       dt * 16 + nb + nn + 1]),
                            rd=["spt", "a_sb"], wr=[("dap", par)]))
                    bx = pb.op("v", lambda e, nb=nb, dbxp=dbxp: e.tensor_tensor(
                        dbxp[:].rearrange("p (n t) -> p n t", n=NG)
                        [:, :, 1:1 + TC],
                        uu[:, :].unsqueeze(1).broadcast_to([P, NG, TC]),
                        brep[:, nb * TC:(nb + NG) * TC]
                        .rearrange("p (n t) -> p n t", n=NG), ALU.mult),
                        rd=["uu", "brep"], wr=[("dx", par)])
                    stc = pb.op("s", lambda e, dt=dt, nb=nb, dbxp=dbxp:
                                e.activation(
                        dbxp[:, 0:NG * SC:SC],
                        hstate[:, dt * 16 + nb:dt * 16 + nb + NG], AF.Copy),
                        rd=["hstate"], wr=[("dx", par)])
                    scn = pb.op("v", lambda e, dap=dap, dbxp=dbxp:
                                e.tensor_tensor_scan(
                        dbxp[:], dap[:], dbxp[:], 0.0, ALU.mult, ALU.add),
                        rd=[("dap", par), ("dx", par)], wr=[("dx", par)],
                        force=[bx, stc])
                    pb.op("v", lambda e, dt=dt, nb=nb, dbxp=dbxp: e.tensor_copy(
                        hstate[:, dt * 16 + nb:dt * 16 + nb + NG],
                        dbxp[:, TC:NG * SC:SC]),
                        rd=[("dx", par)], wr=["hstate"], force=[scn])
                    pb.op("v", lambda e, nb=nb, dbxp=dbxp:
                          e.tensor_tensor(
                        dbxp[:].rearrange("p (n t) -> p n t", n=NG)
                        [:, :, 1:1 + TC],
                        dbxp[:].rearrange("p (n t) -> p n t", n=NG)
                        [:, :, 1:1 + TC],
                        crep[:, nb * TC:(nb + NG) * TC]
                        .rearrange("p (n t) -> p n t", n=NG), ALU.mult),
                        rd=[("dx", par), "crep"], wr=[("dx", par)])
                    emit_trees((par, dbxp, ng))
                pb.op("v", lambda e, dt=dt: e.scalar_tensor_tensor(
                    y_acc[:], xc_bf[:, dt * TC:(dt + 1) * TC],
                    dpar_sb[:, dt:dt + 1], y_acc[:], ALU.mult, ALU.add),
                    rd=[("xc", dt), "dpar_sb", "y_acc"], wr=["y_acc"])
                pb.op("v", lambda e, dt=dt: e.tensor_tensor(
                    y_g[:, dt * TC:(dt + 1) * TC], y_acc[:],
                    sz_bf[:, dt * TC:(dt + 1) * TC], ALU.mult),
                    rd=["y_acc", "sz_bf"], wr=["y_g"])

        def p1_tail(c):
            for o in range(8):
                wb_, wk_ = wslices("woT", o * 128, 128, 16, o % 2)
                for k in range(16):
                    pb.op("t", lambda e, o=o, k=k, wb_=wb_: e.matmul(
                        mm_ps[o % 2][:], wb_[:, k * 128:(k + 1) * 128],
                        y_g[:, k * TC:(k + 1) * TC],
                        start=(k == 0), stop=(k == 15)),
                        rd=[wk_, "y_g"], wr=[("mm", o % 2)])
                pb.op("s", lambda e, o=o: e.activation(
                    oa[:, o * TC:(o + 1) * TC], mm_ps[o % 2][:], AF.Copy),
                    rd=[("mm", o % 2)], wr=["oa"])
            xb, xk, xch = p1_xbuf(c)
            ln_stats(oa, "oa", 8)
            ln_apply(oa, "oa", oa, "oa", 8, res=xb, res_key=xk)
            store_chunk(dstS, dstK, c, oa, "oa")
        p1_front(0)
        for c in range(NCH):
            p1_scan(c)
            if c + 1 < NCH:
                p1_front(c + 1)
            p1_tail(c)
        pb.barrier()
        p1ctx.close()

    if 2 in phases:
        p2ctx = ExitStack()
        sb2, ps2 = mk(p2ctx)
        (srcS, srcK), (dstS, dstK) = stream_of(2)
        xinB_2 = sb2("p2_xinB", (P, 8 * TC))
        xn2 = sb2("p2_xn2", (P, 8 * TC), BF16)
        h_bf = sb2("p2_h", (P, 32 * TC), BF16)
        lstage = sb2("p2_lstage", (P, 128), BF16)
        ltr = sb2("p2_ltr", (P, 128), BF16)
        odd1_bf = sb2("p2_odd1", (P, 32), BF16)
        odd2_bf = sb2("p2_odd2", (P, 32), BF16)
        evn = [sb2(f"p2_evn{i}", (1, 1024), BF16) for i in range(8)]
        u1rows = sb2("p2_u1rows", (4, TC), BF16)
        u1r = sb2("p2_u1r", (1, 4 * TC), BF16)
        v_sb = sb2("p2_vsb", (4, 4 * TC), BF16)
        vr = sb2("p2_vr", (1, 16 * TC), BF16)
        u1_ps = ps2("p2_u1ps", (4, TC))
        v_ps = ps2("p2_vps", (4, TC))

        def p2_xbuf(c):
            return ((xin, "xin", "xia") if c % 2 == 0
                    else (xinB_2, "xinB", "xib"))

        def p2_nbuf(c):
            return (xn_bf, "xnsq") if c % 2 == 0 else (xn2, "xn2")

        def p2_ln_front(c):
            xb, xk, xch = p2_xbuf(c)
            nb_, nk_ = p2_nbuf(c)
            load_chunk(srcS, srcK, c, xb, xk, xch)
            ln_stats(xb, xk, 8, sqb=nb_, sqk=nk_)
            ln_apply(xb, xk, nb_, nk_, 8)

        def p2_body(c):
            xb, xk, xch = p2_xbuf(c)
            nb_, nk_ = p2_nbuf(c)
            if c == 0:
                pb.op("v", lambda e, nb_=nb_: e.tensor_copy(
                    lstage[:].rearrange("p (k t) -> p k t", k=8),
                    nb_[:, 0:8 * TC].rearrange("p (k t) -> p k t", k=8)
                    [:, :, 1:17]),
                    rd=[nk_], wr=["lstage"])
                pb.op("v", lambda e, nb_=nb_: e.tensor_copy(
                    odd1_bf[:].rearrange("p (k t) -> p k t", k=8),
                    nb_[:, 0:8 * TC].rearrange("p (k t) -> p k t", k=8)
                    [:, :, 2:9:2]),
                    rd=[nk_], wr=["odd1"])
                pb.op("v", lambda e, nb_=nb_: e.tensor_copy(
                    odd2_bf[:].rearrange("p (k t) -> p k t", k=8),
                    nb_[:, 0:8 * TC].rearrange("p (k t) -> p k t", k=8)
                    [:, :, 10:17:2]),
                    rd=[nk_], wr=["odd2"])
                pb.op("t", lambda e: e.transpose(
                    rep_ps[0][:, :].bitcast(BF16)[:, 0:128], lstage[:],
                    iden_sb[:]),
                    rd=["lstage", "iden_sb"], wr=["rep0"])
                pb.op("v", lambda e: e.tensor_copy(
                    ltr[:], rep_ps[0][:, :].bitcast(BF16)[:, 0:128]),
                    rd=["rep0"], wr=["ltr"])
                pb.dma(lambda e: e.dma_start(out=lscr, in_=ltr[:]),
                       rd=["ltr"], wr=["lscr"], chan="p0")
                prev = None
                for i in range(8):
                    j = 2 * i if i < 4 else 2 * (i - 4) + 8
                    prev = pb.dma(lambda e, i=i, j=j: e.dma_start(
                        out=evn[i][:].rearrange("x (k p) -> x k p", k=8),
                        in_=lscr.rearrange("(k j) p -> j k p", j=16)
                        [j].unsqueeze(0)),
                        rd=["lscr"], wr=[("evn", i)], deps=[prev], chan="p0")

            for k in range(8):
                pb.op("t", lambda e, k=k, nb_=nb_: e.matmul(
                    u1_ps[:], odd1_bf[:, k * 4:(k + 1) * 4],
                    nb_[:, k * TC:(k + 1) * TC],
                    start=(k == 0), stop=(k == 7)),
                    rd=["odd1", nk_], wr=["u1_ps"])
            pb.op("v", lambda e: e.tensor_scalar(
                u1rows[:], u1_ps[:], scl_sb[0:4, 0:1], None, ALU.mult),
                rd=["u1_ps", "scl_sb"], wr=["u1rows"])
            for i in range(4):
                pb.dma(lambda e, i=i: e.dma_start(
                    out=u1r[0:1, i * TC:(i + 1) * TC], in_=u1rows[i:i + 1, :]),
                    rd=["u1rows"], wr=["u1r"], chan="lx")

            for g in range(8):
                wb_, wk_ = wslices("w1T", g * 512, 512, 8, g % 2)
                for fi in range(4):
                    f = g * 4 + fi
                    i = f // 8
                    for k in range(8):
                        pb.op("t", lambda e, fi=fi, k=k, f=f, wb_=wb_,
                              nb_=nb_: e.matmul(
                            mm_ps[f % 2][:],
                            wb_[:, k * 512 + fi * 128:k * 512 + fi * 128 + 128],
                            nb_[:, k * TC:(k + 1) * TC],
                            start=(k == 0), stop=False),
                            rd=[wk_, nk_], wr=[("mm", f % 2)])
                    pb.op("t", lambda e, f=f, i=i: e.matmul(
                        mm_ps[f % 2][:],
                        evn[i][0:1, (f % 8) * 128:(f % 8) * 128 + 128],
                        u1r[0:1, i * TC:(i + 1) * TC],
                        start=False, stop=True),
                        rd=[("evn", i), "u1r"], wr=[("mm", f % 2)])
                    pb.op("s", lambda e, f=f: e.activation(
                        h_bf[:, f * TC:(f + 1) * TC], mm_ps[f % 2][:], AF.Gelu,
                        bias=b1_sb[:, f:f + 1]),
                        rd=[("mm", f % 2), "b1_sb"], wr=[("h", f)])

        def p2_tail(c):
            xb, xk, xch = p2_xbuf(c)
            for j in range(4):
                for fk in range(8):
                    f = j * 8 + fk
                    pb.op("t", lambda e, f=f: e.matmul(
                        v_ps[:],
                        odd2_bf[:, (f % 8) * 4:(f % 8) * 4 + 4],
                        h_bf[:, f * TC:(f + 1) * TC],
                        start=(f % 8 == 0), stop=(f % 8 == 7)),
                        rd=[("h", f), "odd2"], wr=["v_ps"])
                pb.op("v", lambda e, j=j: e.tensor_scalar(
                    v_sb[:, j * TC:(j + 1) * TC], v_ps[:],
                    scl_sb[0:4, 0:1], None, ALU.mult),
                    rd=["v_ps", "scl_sb"], wr=["v_sb"])
            for i in range(4):
                pb.dma(lambda e, i=i: e.dma_start(
                    out=vr[0:1, i * 4 * TC:(i + 1) * 4 * TC],
                    in_=v_sb[i:i + 1, :]),
                    rd=["v_sb"], wr=["vr"], chan="lx")

            for o in range(8):
                i = o // 2
                wb_, wk_ = wslices("w2T", o * 128, 128, 32, o % 2)
                for k in range(32):
                    pb.op("t", lambda e, o=o, k=k, wb_=wb_: e.matmul(
                        mm_ps[o % 2][:], wb_[:, k * 128:(k + 1) * 128],
                        h_bf[:, k * TC:(k + 1) * TC],
                        start=(k == 0), stop=False),
                        rd=[wk_, ("h", k)], wr=[("mm", o % 2)])
                for j in range(4):
                    pb.op("t", lambda e, o=o, i=i, j=j: e.matmul(
                        mm_ps[o % 2][:],
                        evn[4 + i][0:1,
                                   (o % 2) * 512 + j:(o % 2) * 512 + j + 509:4],
                        vr[0:1, (i * 4 + j) * TC:(i * 4 + j + 1) * TC],
                        start=False, stop=(j == 3)),
                        rd=[("evn", 4 + i), "vr"], wr=[("mm", o % 2)])
                pb.op("v", lambda e, o=o: e.tensor_scalar(
                    oa[:, o * TC:(o + 1) * TC], mm_ps[o % 2][:],
                    b2_sb[:, o:o + 1], None, ALU.add),
                    rd=[("mm", o % 2), "b2_sb"], wr=["oa"])
            nb_, nk_ = p2_nbuf(c)
            ln_stats(oa, "oa", 8, sqb=nb_, sqk=nk_)
            ln_apply(oa, "oa", oa, "oa", 8, res=xb, res_key=xk)
            store_chunk(dstS, dstK, c, oa, "oa")

        p2_ln_front(0)
        for c in range(NCH):
            p2_body(c)
            if c + 1 < NCH:
                p2_ln_front(c + 1)
            p2_tail(c)
        pb.barrier()
        p2ctx.close()

    if 3 in phases:
        p3ctx = ExitStack()
        sb3, ps3 = mk(p3ctx)
        (srcS, srcK), (dstS, dstK) = stream_of(3)
        xinB_3 = sb3("p3_xinB", (P, 8 * TC))
        xn5b = [sb3("p3_xn5a", (P, 8 * TC)), sb3("p3_xn5b", (P, 8 * TC))]
        hhb = [sb3("p3_hha", (P, 8 * TC), BF16), sb3("p3_hhb", (P, 8 * TC), BF16)]
        pad3 = [sb3("p3_pad0", (P, TC + 3), BF16),
                sb3("p3_pad1", (P, TC + 3), BF16)]
        acc3 = [sb3("p3_acc0", (P, TC)), sb3("p3_acc1", (P, TC))]
        xst = [sb3("p3_xst0", (P, TC)), sb3("p3_xst1", (P, TC))]
        srt = [sb3("p3_srt0", (P, TC), BF16), sb3("p3_srt1", (P, TC), BF16)]
        y3_bf = sb3("p3_y", (P, 16 * TC), BF16)
        mm3 = [ps3("p3_mm2", (P, TC)), ps3("p3_mm3", (P, TC))]

        def p3_xbuf(c):
            return ((xin, "xin", "xia") if c % 2 == 0
                    else (xinB_3, "xinB", "xib"))

        def p3_front(c):
            par = c % 2
            xb, xk, xch = p3_xbuf(c)
            load_chunk(srcS, srcK, c, xb, xk, xch)
            ln_stats(xb, xk, 8)
            ln_apply(xb, xk, xn5b[par], ("xn5", par), 8)
            ln_stats(xn5b[par], ("xn5", par), 8, rms_only=True)
            for k in range(8):
                pb.op("v", lambda e, k=k, par=par: e.scalar_tensor_tensor(
                    hhb[par][:, k * TC:(k + 1) * TC],
                    xn5b[par][:, k * TC:(k + 1) * TC],
                    nw_sb[:, k:k + 1], rep_ps[1][:], ALU.mult, ALU.mult),
                    rd=[("xn5", par), "nw_sb", "rep1"], wr=[("hh", par)])

        def p3_body(c):
            par = c % 2
            hh = hhb[par]
            hk = ("hh", par)
            for g in range(4):
                wslices("winL1", g * 512, 512, 8, 0)
                wslices("wresL1", g * 512, 512, 8, 1)
                for fi in range(4):
                    f = g * 4 + fi
                    fp = f % 2
                    for k in range(8):
                        pb.op("t", lambda e, fi=fi, k=k, hh=hh, fp=fp: e.matmul(
                            mm_ps[fp][:],
                            wsl[:, k * 512 + fi * 128:k * 512 + fi * 128 + 128],
                            hh[:, k * TC:(k + 1) * TC],
                            start=(k == 0), stop=(k == 7)),
                            rd=["wsla", hk], wr=[("mm", fp)])
                    for k in range(8):
                        pb.op("t", lambda e, fi=fi, k=k, hh=hh, fp=fp: e.matmul(
                            mm3[fp][:],
                            wslB[:, k * 512 + fi * 128:k * 512 + fi * 128 + 128],
                            hh[:, k * TC:(k + 1) * TC],
                            start=(k == 0), stop=(k == 7)),
                            rd=["wslb", hk], wr=[("mm3", fp)])
                    pb.op("s", lambda e, fp=fp: e.activation(
                        pad3[fp][:, 3:3 + TC], mm_ps[fp][:], AF.Copy),
                        rd=[("mm", fp)], wr=[("pad3", fp)])
                    pb.op("v", lambda e, f=f, fp=fp: e.tensor_copy(
                        pad3[fp][:, 0:3], halo1[:, f * 3:f * 3 + 3]),
                        rd=["halo1"], wr=[("pad3", fp)])
                    pb.op("v", lambda e, f=f, fp=fp: e.tensor_scalar(
                        acc3[fp][:], pad3[fp][:, 0:TC],
                        cw1_sb[:, 4 * f:4 * f + 1], None, ALU.mult),
                        rd=[("pad3", fp), "cw1_sb"], wr=[("acc3", fp)])
                    for kk in range(1, 4):
                        pb.op("v", lambda e, f=f, kk=kk, fp=fp:
                              e.scalar_tensor_tensor(
                            acc3[fp][:], pad3[fp][:, kk:kk + TC],
                            cw1_sb[:, 4 * f + kk:4 * f + kk + 1],
                            acc3[fp][:], ALU.mult, ALU.add),
                            rd=[("pad3", fp), ("acc3", fp), "cw1_sb"],
                            wr=[("acc3", fp)])
                    pb.op("v", lambda e, f=f, fp=fp: e.tensor_copy(
                        halo1[:, f * 3:f * 3 + 3], pad3[fp][:, TC:TC + 3]),
                        rd=[("pad3", fp)], wr=["halo1"])
                    pb.op("s", lambda e, f=f, fp=fp: e.activation(
                        xst[fp][:], acc3[fp][:], AF.Silu,
                        bias=cb1_sb[:, f:f + 1]),
                        rd=[("acc3", fp), "cb1_sb"], wr=[("xst", fp)])
                    pb.op("s", lambda e, fp=fp: e.activation(
                        srt[fp][:], mm3[fp][:], AF.Silu),
                        rd=[("mm3", fp)], wr=[("srt", fp)])
                    pb.op("v", lambda e, f=f, fp=fp: e.tensor_tensor(
                        y3_bf[:, f * TC:(f + 1) * TC], xst[fp][:], srt[fp][:],
                        ALU.mult),
                        rd=[("xst", fp), ("srt", fp)], wr=["y3_bf"])

        def p3_tail(c):
            par = c % 2
            xb, xk, xch = p3_xbuf(c)
            for o in range(8):
                wb_, wk_ = wslices("woutL1", o * 128, 128, 16, o % 2)
                for k in range(16):
                    pb.op("t", lambda e, o=o, k=k, wb_=wb_: e.matmul(
                        mm_ps[o % 2][:], wb_[:, k * 128:(k + 1) * 128],
                        y3_bf[:, k * TC:(k + 1) * TC],
                        start=(k == 0), stop=(k == 15)),
                        rd=[wk_, "y3_bf"], wr=[("mm", o % 2)])
                pb.op("v", lambda e, o=o, par=par: e.tensor_tensor(
                    oa[:, o * TC:(o + 1) * TC], mm_ps[o % 2][:],
                    xn5b[par][:, o * TC:(o + 1) * TC], ALU.add),
                    rd=[("mm", o % 2), ("xn5", par)], wr=["oa"])
            ln_stats(oa, "oa", 8)
            ln_apply(oa, "oa", oa, "oa", 8, res=xb, res_key=xk)
            store_chunk(dstS, dstK, c, oa, "oa")

        p3_front(0)
        for c in range(NCH):
            p3_body(c)
            if c + 1 < NCH:
                p3_front(c + 1)
            p3_tail(c)
        pb.barrier()
        p3ctx.close()

    if 4 in phases:
        p4ctx = ExitStack()
        sb4, ps4 = mk(p4ctx)
        (srcS, srcK), (dstS, dstK) = stream_of(4)
        xinB_4 = sb4("p4_xinB", (P, 8 * TC))
        xn4 = sb4("p4_xn2", (P, 8 * TC), BF16)
        h2a = sb4("p4_h2a", (P, 32 * TC), BF16)
        h2b = sb4("p4_h2b", (P, 32 * TC), BF16)
        mem = sb4("p4_mem", (P, 8 * TC))
        s4b = sb4("p4_s", (P, 8 * TC), BF16)
        mm4 = [ps4("p4_mm2", (P, TC)), ps4("p4_mm3", (P, TC))]
        acc4 = sb4("p4_acc", (P, 8 * TC), BF16)

        def p4_xbuf(c):
            return ((xin, "xin", "xia") if c % 2 == 0
                    else (xinB_4, "xinB", "xib"))

        def p4_nbuf(c):
            return (xn_bf, "xnsq") if c % 2 == 0 else (xn4, "xn2")

        def p4_front(c):
            xb, xk, xch = p4_xbuf(c)
            nb_, nk_ = p4_nbuf(c)
            load_chunk(srcS, srcK, c, xb, xk, xch)
            ln_stats(xb, xk, 8, sqb=nb_, sqk=nk_)
            ln_apply(xb, xk, nb_, nk_, 8)

        def p4_fc(c):
            nb_, nk_ = p4_nbuf(c)
            h2 = h2a if c % 2 == 0 else h2b
            hk = ("h2", c % 2)
            for g in range(8):
                wb_, wk_ = wslices("wfcT", g * 512, 512, 8, g % 2)
                for fi in range(4):
                    f = g * 4 + fi
                    for k in range(8):
                        pb.op("t", lambda e, fi=fi, k=k, f=f, wb_=wb_,
                              nb_=nb_: e.matmul(
                            mm_ps[f % 2][:],
                            wb_[:, k * 512 + fi * 128:k * 512 + fi * 128 + 128],
                            nb_[:, k * TC:(k + 1) * TC],
                            start=(k == 0), stop=(k == 7)),
                            rd=[wk_, nk_], wr=[("mm", f % 2)])
                    pb.op("s", lambda e, f=f: e.activation(
                        tmpf[:], mm_ps[f % 2][:], AF.Relu),
                        rd=[("mm", f % 2)], wr=["tmpf"])
                    pb.op("s", lambda e, f=f, h2=h2: e.activation(
                        h2[:, f * TC:(f + 1) * TC], tmpf[:], AF.Square),
                        rd=["tmpf"], wr=[hk])

        def p4_tail(c):
            xb, xk, xch = p4_xbuf(c)
            h2 = h2a if c % 2 == 0 else h2b
            hk = ("h2", c % 2)
            for o in range(8):
                wb_, wk_ = wslices("wprT", o * 128, 128, 32, o % 2)
                for k in range(32):
                    pb.op("t", lambda e, o=o, k=k, wb_=wb_, h2=h2: e.matmul(
                        mm4[o % 2][:], wb_[:, k * 128:(k + 1) * 128],
                        h2[:, k * TC:(k + 1) * TC],
                        start=(k == 0), stop=(k == 31)),
                        rd=[wk_, hk], wr=[("mm4", o % 2)])
                pb.op("s", lambda e, o=o: e.activation(
                    oa[:, o * TC:(o + 1) * TC], mm4[o % 2][:], AF.Copy),
                    rd=[("mm4", o % 2)], wr=["oa"])
            W = 8 * TC
            pb.op("v", lambda e: e.tensor_copy(mem[:, 0:W], oa[:, 0:W]),
                  rd=["oa"], wr=["mem"])
            pb.op("v", lambda e: e.tensor_single_scalar(
                acc4[:, 0:W], mem[:, 0:W], 1.0, ALU.is_gt),
                rd=["mem"], wr=["acc4"])
            pb.op("v", lambda e: e.tensor_copy(s4b[:, 0:W], acc4[:, 0:W]),
                  rd=["acc4"], wr=["s4"])
            for k in range(4):
                pb.op("v", lambda e: e.scalar_tensor_tensor(
                    mem[:, 0:W], mem[:, 0:W], 0.9, oa[:, 0:W],
                    ALU.mult, ALU.add),
                    rd=["mem", "oa"], wr=["mem"])
                pb.op("v", lambda e: e.tensor_tensor(
                    mem[:, 0:W], mem[:, 0:W], s4b[:, 0:W], ALU.subtract),
                    rd=["mem", "s4"], wr=["mem"])
                pb.op("v", lambda e: e.tensor_single_scalar(
                    s4b[:, 0:W], mem[:, 0:W], 1.0, ALU.is_gt),
                    rd=["mem"], wr=["s4"])
                pb.op("v", lambda e: e.tensor_tensor(
                    acc4[:, 0:W], acc4[:, 0:W], s4b[:, 0:W], ALU.add),
                    rd=["acc4", "s4"], wr=["acc4"])
            pb.op("v", lambda e: e.tensor_single_scalar(
                oa[:, 0:W], acc4[:, 0:W], 0.2, ALU.mult),
                rd=["acc4"], wr=["oa"])
            nb_, nk_ = p4_nbuf(c)
            ln_stats(oa, "oa", 8, sqb=nb_, sqk=nk_)
            ln_apply(oa, "oa", oa, "oa", 8, res=xb, res_key=xk)
            store_chunk(dstS, dstK, c, oa, "oa")

        p4_front(0)
        p4_fc(0)
        for c in range(NCH):
            if c + 1 < NCH:
                p4_front(c + 1)
                p4_fc(c + 1)
            p4_tail(c)
        pb.barrier()
        p4ctx.close()

    pb.barrier()
    pb.finalize()

    with ExitStack() as sctx:
        sems = {}
        for name in list(ENGS) + sorted(pb.chans):
            sems[name] = sctx.enter_context(nc.semaphore("sem_" + str(name)))
        with nc.Block() as block:
            @block.sync
            def _(e):
                pb.replay("q", e, sems)
                for chname in sorted(pb.chans):
                    tot = pb.totals.get(chname, 0)
                    if tot:
                        e.wait_ge(sems[chname], tot)

            @block.tensor
            def _(e):
                pb.replay("t", e, sems)

            @block.vector
            def _(e):
                pb.replay("v", e, sems)

            @block.scalar
            def _(e):
                pb.replay("s", e, sems)

            @block.gpsimd
            def _(e):
                pb.replay("g", e, sems)

    ctx.close()
    return nc


def prep_inputs(inputs):
    T = lambda a: np.ascontiguousarray(np.asarray(a).T)

    def pmajor(v):
        v = np.asarray(v)
        tail = v.shape[1:]
        k = v.shape[0] // P
        return np.ascontiguousarray(
            v.reshape((k, P) + tail).swapaxes(0, 1).reshape(P, -1))

    common = {
        "wioT": T(inputs["L0_attn_W_in"]),
        "wxT": pmajor(np.asarray(inputs["L0_attn_W_x"]).T),
        "cw0": pmajor(np.asarray(inputs["L0_attn_conv_w"])[:, 0, :]),
        "cb0": pmajor(inputs["L0_attn_conv_b"]),
        "wdt": pmajor(inputs["L0_attn_w_dt"]),
        "bdt": pmajor(inputs["L0_attn_b_dt"]),
        "alog": pmajor(inputs["L0_attn_A_log"]),
        "dpar": pmajor(inputs["L0_attn_D_param"]),
        "woT": T(inputs["L0_attn_W_out"]),
        "w1T": T(inputs["L0_ffn_W1"]),
        "b1": pmajor(inputs["L0_ffn_b1"]),
        "w2T": T(inputs["L0_ffn_W2"]),
        "b2": pmajor(inputs["L0_ffn_b2"]),
        "scl0": np.asarray(inputs["L0_ffn_scale"], np.float32).reshape(1, 1),
        "nw1": pmajor(inputs["L1_attn_norm_w"]),
        "winL1": np.ascontiguousarray(inputs["L1_attn_w_in"]),
        "wresL1": np.ascontiguousarray(inputs["L1_attn_w_res"]),
        "cw1": pmajor(inputs["L1_attn_conv_w"]),
        "cb1": pmajor(inputs["L1_attn_conv_b"]),
        "woutL1": np.ascontiguousarray(inputs["L1_attn_w_out"]),
        "wfcT": T(inputs["L1_ffn_W_fc"]),
        "wprT": T(inputs["L1_ffn_W_proj"]),
    }
    sel = np.zeros((33, 33 * 128), dtype=BF)
    for n in range(33):
        sel[n, n * 128:(n + 1) * 128] = 1
    common["sel33"] = sel
    common["ones1"] = np.ones((1, 128), dtype=BF)
    common["onesc"] = np.ones((128, 1), dtype=BF)
    common["iden"] = np.eye(128, dtype=np.float32).astype(BF)
    x = np.asarray(inputs["L0_attn_x"])
    in_maps = []
    for b in range(8):
        m = dict(common)
        m["xT"] = np.ascontiguousarray(x[b].T)
        in_maps.append(m)
    return in_maps


def kernel(**inputs):
    in_maps = prep_inputs(inputs)
    nc = build_program(phases=(1, 2, 3, 4))
    res = run_bass_kernel_spmd(nc, in_maps, core_ids=list(range(8)))
    out = np.stack([np.asarray(r["out"]).T for r in res.results])
    return np.ascontiguousarray(out.astype(np.float32))
